# version 01 (final)
# baseline (speedup 1.0000x reference)
import numpy as np
import ml_dtypes
import concourse.bass as bass
import concourse.mybir as mybir
from concourse.bass_utils import run_bass_kernel_spmd
from contextlib import ExitStack

F32 = mybir.dt.float32
BF16 = mybir.dt.bfloat16
AF = mybir.ActivationFunctionType
ALU = mybir.AluOpType
BF = ml_dtypes.bfloat16

B, L, D = 4, 8192, 1024
TOK = 4096
HALO_A = 8
HALO_F = 272
CBN = 8
ST = 512
NST = TOK // ST
GRP = 128
NGRP = TOK // GRP
HDN = 4
DKC = 2
CH = 32
NCH = 4
NL = 6
DI = 2730
NJ = 22
JW = [128] * 21 + [42]
JOFF = [128 * i for i in range(21)] + [2688]
HF = 16
DH = 64
WIN = 256
NEG = -3.0e38
XW = ST + 8
QW = ST + 3
ZW = HALO_F + TOK

PHASE_B = True
DBG = False
DBGZ = False



def _perm_dn():
    p = []
    for h in range(HDN):
        base = h * 256
        p += [base + j for j in range(0, 256, 2)]
        p += [base + j for j in range(1, 256, 2)]
    return np.array(p, dtype=np.int64)


def _dn_tables(dtype):
    j = np.arange(128, dtype=np.float64)
    inv = 1.0 / 10000.0 ** (2.0 * j / 256.0)
    t = np.arange(L, dtype=np.float64)
    ang = t[:, None] * inv[None, :]
    return (np.ascontiguousarray(np.sin(ang).T.astype(dtype)),
            np.ascontiguousarray(np.cos(ang).T.astype(dtype)))


def _ffn_tables(dtype):
    half = np.arange(0, DH, 2, dtype=np.float64)
    inv = 1.0 / 10000.0 ** (half / DH)
    t = np.arange(-HALO_F, L, dtype=np.float64)
    fr = t[:, None] * inv[None, :]
    bs = (half + 0.4 * DH) / (1.4 * DH)
    pw = (t - L // 2) / (WIN // 2)
    sc = bs[None, :] ** pw[:, None]
    cos, sin = np.cos(fr), np.sin(fr)
    def dup(v):
        v64 = np.concatenate([v, v], axis=1)
        v128 = np.concatenate([v64, v64], axis=1)
        return np.ascontiguousarray(v128.T.astype(dtype))
    return (dup(cos * sc * 8.0), dup(sin * sc * 8.0),
            dup(cos / sc), dup(sin / sc))


def _ffn_masks(first_half):
    ii = np.arange(256)[:, None]
    jj = np.arange(512)[None, :]
    reg = np.zeros((256, 512), np.float32)
    reg[(jj < 256) & (jj < ii)] = NEG
    reg[(jj >= 256) & (jj - 256 > ii)] = NEG
    w0 = reg.copy()
    if first_half:
        w0[:, :256] = NEG
    return np.stack([w0, reg]).astype(np.float32)


def _ffn_masks_t(first_half):
    fm = _ffn_masks(first_half)
    fmt = np.zeros((128, 2048), np.float32)
    for slot in range(2):
        for jc in range(4):
            c0 = (slot * 4 + jc) * 256
            fmt[:, c0:c0 + 256] = fm[slot, :, jc * 128:(jc + 1) * 128].T
    return fmt


def _consts():
    idm = np.eye(128, dtype=BF)
    slm = np.zeros((128, 128), dtype=BF)
    for c in range(4):
        b = c * 32
        for i in range(32):
            slm[b + i, b:b + i] = 1
    ilm = np.zeros((32, 32), dtype=BF)
    for jx in range(32):
        ilm[jx, jx:] = 1
    gsel = np.zeros((128, 4), dtype=BF)
    for c in range(4):
        gsel[c * 32:(c + 1) * 32, c] = 1.0 / 32.0
    ones1 = np.ones((1, 128), dtype=BF)
    ones_c = np.ones((128, 1), dtype=BF)
    idmf = np.eye(128, dtype=np.float32)
    ones1f = np.ones((1, 128), dtype=np.float32)
    esel = np.zeros((128, CBN * 16), dtype=BF)
    rsel = np.zeros((16, CBN * 128), dtype=BF)
    for kc in range(CBN):
        for r in range(128):
            h = kc * 2 + r // 64
            esel[r, kc * 16 + h] = 1
            rsel[h, kc * 128 + r] = 1
    prot = np.zeros((128, 128), dtype=BF)
    for blk in range(2):
        o = blk * 64
        for d in range(32):
            prot[o + d + 32, o + d] = -1
            prot[o + d, o + d + 32] = 1
    pinv = np.zeros((128, 4 * 128), dtype=BF)
    for d in range(256):
        nw = d // 2 if d % 2 == 0 else 128 + (d - 1) // 2
        ic, k = nw // 128, nw % 128
        r, m = d // 128, d % 128
        pinv[k, (ic * 2 + r) * 128 + m] = 1
    return dict(idm=idm, slm=slm, ilm=ilm, gsel=gsel, ones1=ones1,
                ones_c=ones_c, esel=esel, rsel=rsel, prot=prot, pinv=pinv,
                idmf=idmf, ones1f=ones1f)


def _chunk_cols(vec, parts=128):
    n = vec.shape[0]
    ncol = (n + parts - 1) // parts
    out = np.zeros((parts, ncol), np.float32)
    for c in range(ncol):
        seg = vec[c * parts:(c + 1) * parts]
        out[:seg.shape[0], c] = seg
    return out


def _stage(inputs):
    A = inputs
    f4 = np.float32
    x = np.ascontiguousarray(A["L0_attn_hidden_states"], dtype=f4)
    perm = _perm_dn()
    wq = np.ascontiguousarray(np.asarray(A["L0_attn_Wq"], f4)[:, perm])
    wk = np.ascontiguousarray(np.asarray(A["L0_attn_Wk"], f4)[:, perm])
    wv = np.ascontiguousarray(A["L0_attn_Wv"], dtype=f4)
    wbg = np.ascontiguousarray(
        np.concatenate([A["L0_attn_Wb"], A["L0_attn_Wgamma"]], axis=1).astype(f4))

    def conv_cols(cw, permute):
        cwp = cw[perm] if permute else cw
        out = np.zeros((128, CBN * 4), f4)
        for cb in range(CBN):
            out[:, cb * 4:(cb + 1) * 4] = cwp[cb * 128:(cb + 1) * 128, :]
        return out
    convq = conv_cols(np.asarray(A["L0_attn_conv_q"], f4), True)
    convk = conv_cols(np.asarray(A["L0_attn_conv_k"], f4), True)
    convv = conv_cols(np.asarray(A["L0_attn_conv_v"], f4), False)
    rotlog = np.asarray(A["L0_attn_rot_logit"], f4).reshape(1, 4)
    onw = np.zeros((128, 2), f4)
    onw[:, 0] = A["L0_attn_o_norm_w"][:128]
    onw[:, 1] = A["L0_attn_o_norm_w"][128:]
    wo_dn = np.ascontiguousarray(A["L0_attn_Wo"], dtype=f4)
    fcw = np.asarray(A["L0_attn_frac_conv_w"], f4)
    fracw = np.zeros((128, CBN * 8), f4)
    for cb in range(CBN):
        for i in range(4):
            fracw[:, cb * 8 + i] = fcw[i, cb * 128:(cb + 1) * 128, 0]
            fracw[:, cb * 8 + 4 + i] = fcw[i, cb * 128:(cb + 1) * 128, 1]
    fgw = np.zeros((128, 8), f4)
    for cb in range(CBN):
        fgw[:, cb] = np.asarray(A["L0_attn_frac_gate_w"], f4)[cb * 128:(cb + 1) * 128, 0]
    fgb = np.asarray(A["L0_attn_frac_gate_b"], f4).reshape(1, 1)
    mixw = np.zeros((128, 8), f4)
    for cb in range(CBN):
        mixw[:, cb] = np.asarray(A["L0_attn_mixer_norm_w"], f4)[cb * 128:(cb + 1) * 128]

    sdn_f, cdn_f = _dn_tables(x.dtype)
    cs_f, ss_f, ci_f, si_f = _ffn_tables(x.dtype)

    wqkv = np.ascontiguousarray(A["L0_ffn_Wqkv"], dtype=f4)
    wo_ff = np.ascontiguousarray(A["L0_ffn_Wo"], dtype=f4)
    w1 = np.ascontiguousarray(A["L0_ffn_W1"], dtype=f4)
    w2 = np.ascontiguousarray(A["L0_ffn_W2"], dtype=f4)
    qs_rep = np.stack([_chunk_cols(np.tile(np.asarray(A["L0_ffn_q_scale"][l], f4), HF)) for l in range(NL)])
    ks_rep = np.stack([_chunk_cols(np.tile(np.asarray(A["L0_ffn_k_scale"][l], f4), HF)) for l in range(NL)])
    lng = np.stack([_chunk_cols(np.asarray(A["L0_ffn_ln_g"][l], f4)) for l in range(NL)])
    lnbc = np.stack([_chunk_cols(np.asarray(A["L0_ffn_ln_b"][l], f4)) for l in range(NL)])
    b2c = np.stack([_chunk_cols(np.asarray(A["L0_ffn_b2"][l], f4)) for l in range(NL)])
    b1c = np.zeros((NL, 128, 44), f4)
    for l in range(NL):
        for half in range(2):
            for j in range(NJ):
                seg = np.asarray(A["L0_ffn_b1"][l], f4)[half * DI + JOFF[j]: half * DI + JOFF[j] + JW[j]]
                b1c[l, :JW[j], half * NJ + j] = seg

    C = _consts()
    in_maps = []
    for c in range(8):
        b, s = c // 2, c % 2
        t0 = s * TOK
        xb = x[b]
        xcm = np.zeros((CBN, 128, HALO_A + TOK), f4)
        lo = max(0, t0 - HALO_A)
        src = xb[lo:t0 + TOK].T.reshape(CBN, 128, -1)
        xcm[:, :, (HALO_A + TOK) - src.shape[2]:] = src
        m = dict(
            x_cm=np.ascontiguousarray(xcm),
            wq=wq, wk=wk, wv=wv, wbg=wbg,
            convq=convq, convk=convk, convv=convv, rotlog=rotlog,
            onw=onw, wo_dn=wo_dn, fracw=fracw, fgw=fgw, fgb=fgb, mixw=mixw,
            sdn=np.ascontiguousarray(sdn_f[:, t0:t0 + TOK]),
            cdn=np.ascontiguousarray(cdn_f[:, t0:t0 + TOK]),
            cs=np.ascontiguousarray(cs_f[:, t0:t0 + TOK + HALO_F]),
            ss=np.ascontiguousarray(ss_f[:, t0:t0 + TOK + HALO_F]),
            ci=np.ascontiguousarray(ci_f[:, t0:t0 + TOK + HALO_F]),
            si=np.ascontiguousarray(si_f[:, t0:t0 + TOK + HALO_F]),
            fmask=_ffn_masks(first_half=(s == 0)),
            fmask_t=_ffn_masks_t(first_half=(s == 0)),
            cvals=np.repeat(np.array([[1e-5, 1e-6, 0.0, 0.0]], f4), 128, axis=0),
            s0m=np.full((128, 1), 0.0 if s == 0 else 1.0, f4),
            hm=np.full((128, 1), 0.0 if s == 0 else 1.0, f4),
            wqkv=wqkv, wo_ff=wo_ff, w1=w1, w2=w2,
            qsr=qs_rep, ksr=ks_rep, lng=lng, lnbc=lnbc, b2c=b2c, b1c=b1c,
            **C,
        )
        in_maps.append(m)
    return in_maps



class Ctr:
    __slots__ = ("h", "v")
    def __init__(self, h):
        self.h = h
        self.v = 0
    def inc(self, n):
        self.v += n
        return self.v


def _w(eng, c, val=None):
    def f(e, h=c.h, v=(c.v if val is None else val)):
        e.wait_ge(h, v)
    return (eng, f)


def _dma(q, out, in_, ctr, **kw):
    ctr.inc(16)
    def f(e, out=out, in_=in_, kw=kw, h=ctr.h):
        e.dma_start(out=out, in_=in_, **kw).then_inc(h, 16)
    return (q, f)


def _op(eng, meth, *args, inc=None, **kw):
    if inc is not None:
        inc.inc(1)
    def f(e, meth=meth, args=args, kw=kw, inc=inc):
        r = getattr(e, meth)(*args, **kw)
        if inc is not None:
            r.then_inc(inc.h, 1)
    return (eng, f)


class Banks:
    def __init__(self):
        self.g = {}
    def wait(self, name):
        t = self.g.get(name)
        if t is None:
            return []
        return [_w("pe", t[0], t[1])]
    def done(self, name, ctr):
        self.g[name] = (ctr, ctr.v)


class TView:
    __slots__ = ("t", "off", "w", "pmax")
    def __init__(self, t, off, w, pmax=128):
        self.t = t
        self.off = off
        self.w = w
        self.pmax = pmax
    def __getitem__(self, idx):
        p, c = idx
        if isinstance(p, slice) and p.start is None and p.stop is None:
            p = slice(0, self.pmax)
        if isinstance(c, slice):
            a = 0 if c.start is None else c.start
            b = self.w if c.stop is None else c.stop
        else:
            a, b = c, c + 1
        return self.t[p, a + self.off: b + self.off]



def _build_nc(phase_b=PHASE_B):
    nc = bass.Bass("TRN2", target_bir_lowering=False, debug=False)
    dt = nc.dram_tensor
    I, O, N = "ExternalInput", "ExternalOutput", "Internal"
    ap = {}

    def di(name, shape, dty=F32, kind=I):
        ap[name] = dt(name, shape, dty, kind=kind).ap()

    di("x_cm", [CBN, 128, HALO_A + TOK])
    di("y_cm", [CBN, 128, TOK], kind=O)
    di("wq", [1024, 1024]); di("wk", [1024, 1024]); di("wv", [1024, 1024])
    di("wbg", [1024, 8]); di("convq", [128, 32]); di("convk", [128, 32])
    di("convv", [128, 32]); di("rotlog", [1, 4]); di("onw", [128, 2])
    di("wo_dn", [1024, 1024]); di("fracw", [128, 64]); di("fgw", [128, 8])
    di("fgb", [1, 1]); di("mixw", [128, 8])
    di("sdn", [128, TOK]); di("cdn", [128, TOK])
    for n_ in ("cs", "ss", "ci", "si"):
        di(n_, [128, TOK + HALO_F])
    di("fmask", [2, 256, 512])
    di("fmask_t", [128, 2048])
    di("s0m", [128, 1]); di("hm", [128, 1]); di("cvals", [128, 4])
    di("wqkv", [NL, 1024, 3072]); di("wo_ff", [NL, 1024, 1024])
    di("w1", [NL, 1024, 2 * DI]); di("w2", [NL, DI, 1024])
    di("qsr", [NL, 128, 8]); di("ksr", [NL, 128, 8]); di("lng", [NL, 128, 8])
    di("lnbc", [NL, 128, 8]); di("b2c", [NL, 128, 8]); di("b1c", [NL, 128, 44])
    di("idm", [128, 128], BF16); di("slm", [128, 128], BF16)
    di("ilm", [32, 32], BF16); di("gsel", [128, 4], BF16)
    di("ones1", [1, 128], BF16); di("ones_c", [128, 1], BF16)
    di("esel", [128, CBN * 16], BF16)
    di("rsel", [16, CBN * 128], BF16); di("prot", [128, 128], BF16)
    di("pinv", [128, 4 * 128], BF16)
    di("idmf", [128, 128]); di("ones1f", [1, 128])

    DK = O if DBG else N
    di("zf32", [CBN, 128, TOK], kind=DK)
    di("xa_d", [CBN, 128, HALO_A + TOK], BF16, kind=DK)
    di("sp_u", [NGRP, HDN, 128, 256], BF16, kind=DK)
    di("sp_w", [NGRP, HDN, 128, 256], BF16, kind=DK)
    di("sp_k", [NGRP, HDN, 128, 256], BF16, kind=DK)
    di("sp_q", [NGRP, HDN, 128, 256], BF16, kind=DK)
    di("sp_at", [NGRP, HDN, 32, 128], BF16, kind=DK)
    di("ag_s_i", [128, HDN * 512], kind=N)
    if DBG:
        di("dbg_s", [128, HDN * 512], kind=O)
        di("dbg_o", [NGRP, HDN, 32, 1024], BF16, kind=O)
        di("dbg_fr", [NST, 128, CBN * 512], kind=O)
        di("dbg_gt", [NST, 1, 512], BF16, kind=O)
        di("dbg_sm", [128, 80], kind=O)
        di("dbg_s2", [128, HDN * 512], kind=O)
        di("dbg_ago", [256, HDN * 512], kind=O)
    if DBGZ:
        di("dbg_z", [2 * NL, CBN, 128, TOK], kind=O)
        di("dbg_q", [128, 4096], BF16, kind=O)
        di("dbg_k", [128, 6144], BF16, kind=O)
        di("dbg_v", [128, 6144], BF16, kind=O)
        di("dbg_at", [128, 1024], BF16, kind=O)
        di("dbg_sc", [128, 8], kind=O)
        di("dbg_ao", [128, 4096], BF16, kind=O)
        di("dbg_zh", [128, CBN * 280], BF16, kind=O)
        di("dbg_kh", [128, 4096], BF16, kind=O)
        di("dbg_nr", [128, 1024], kind=O)
    di("ag_s_o", [256, HDN * 512], kind=N)
    di("ag_x_i", [128, CBN * 258], kind=N)
    di("ag_x_o", [256, CBN * 258], kind=N)
    di("ag_z_i", [128, CBN * 258], BF16, kind=N)
    di("ag_z_o", [256, CBN * 258], BF16, kind=N)

    stack = ExitStack()
    stack_a = ExitStack()
    sb = {}
    def sbuf(name, shape, dty):
        sb[name] = stack.enter_context(nc.sbuf_tensor(name, shape, dty))
    def sbuf_a(name, shape, dty):
        sb[name] = stack_a.enter_context(nc.sbuf_tensor(name, shape, dty))
    ps = {}
    def psum(name, shape, dty=F32):
        ps[name] = stack.enter_context(nc.psum_tensor(name, shape, dty))

    sbuf("z_bf", [128, CBN * ZW], BF16)
    for n_, shp, d_ in (
        ("c_idm", [128, 128], BF16),
        ("c_ones", [1, 128], BF16), ("c_onesc", [128, 1], BF16),
        ("c_esel", [128, CBN * 16], BF16), ("c_rsel", [16, CBN * 128], BF16),
        ("c_prot", [128, 128], BF16),
        ("c_idmf", [128, 128], F32), ("c_onesf", [1, 128], F32),
        ("s0m_s", [128, 1], F32),
        ("hm_s", [128, 1], F32),
        ("cv", [128, 4], F32),
    ):
        sbuf(n_, shp, d_)

    sbuf("PB", [128, 20992], BF16)
    sbuf("wpA", [128, 8192], BF16)
    sbuf("row1", [1, 1024], F32)
    sbuf("rowm", [1, 1024], F32)
    sbuf("sq", [128, 528], BF16)
    sbuf("sq2", [128, 528], BF16)
    for n_, shp, d_ in (
        ("c_slm", [128, 128], BF16), ("c_ilm", [32, 32], BF16),
        ("c_gsel", [128, 4], BF16), ("c_pinv", [128, 4 * 128], BF16),
        ("mixc", [128, 4], F32), ("imixc", [128, 4], F32),
        ("convs", [128, 96], F32), ("wbg_b", [128, 64], BF16),
        ("small", [128, 16], F32), ("fgw_b", [128, 8], BF16),
        ("fracw_s", [128, 64], F32), ("mixw_s", [128, 8], F32),
        ("onw_s", [128, 2], F32), ("fgb_s", [1, 1], F32),
        ("rl_s", [1, 4], F32), ("rl_b", [1, 4], BF16),
        ("bgf", [128, 4 * 8], F32), ("bgb", [128, 4 * 8], BF16),
        ("nrow", [1, ST], F32), ("osq", [32, 256], BF16),
        ("orms", [32, 16], F32),
    ):
        sbuf_a(n_, shp, d_)
    sbuf_a("S", [128, HDN * 512], F32)
    sbuf_a("S_bf", [128, HDN * 512], BF16)
    sbuf_a("gcol", [128, NGRP * 16], F32)
    sbuf_a("PF1", [128, CBN * 264], F32)
    sbuf_a("PF2", [128, CBN * QW], F32)
    sbuf_a("PF3", [128, 4160], F32)
    sbuf_a("xat", [128, CBN * XW], BF16)
    sbuf_a("cr3", [128, 80], F32)
    sb["xf"] = TView(sb["PF1"], 0, CBN * 264)
    sb["qpre"] = TView(sb["PF2"], 0, CBN * QW)
    sb["fr"] = TView(sb["PF2"], 0, 4096)
    sb["sct"] = TView(sb["PF3"], 0, 2 * ST)
    sb["cmsm"] = TView(sb["PF3"], 1024, 2 * ST)
    sb["rsc"] = TView(sb["PF3"], 2048, 4 * ST)
    sb["x1t"] = TView(sb["PF3"], 0, 4160)
    sb["agt"] = TView(sb["PF3"], 0, 2048)
    sb["qh"] = TView(sb["PB"], 0, CBN * ST)
    sb["kh"] = TView(sb["PB"], 4096, CBN * ST)
    sb["vh"] = TView(sb["PB"], 8192, CBN * ST)
    sb["tb"] = TView(sb["PB"], 12288, 4 * 512)
    sb["kT"] = TView(sb["PB"], 14336, 4 * 256)
    sb["wT"] = TView(sb["PB"], 15360, 4 * 256)
    sb["Lt"] = TView(sb["PB"], 16384, 4 * 128)
    sb["LtT"] = TView(sb["PB"], 16896, 4 * 128)
    sb["atT"] = TView(sb["PB"], 17408, 4 * 128, pmax=32)
    sb["u3"] = TView(sb["PB"], 17920, 4 * 256, pmax=32)
    sb["k3"] = TView(sb["PB"], 18944, 4 * 256, pmax=32)
    sb["utb"] = TView(sb["PB"], 19968, 4 * 256, pmax=128)
    sb["l_u"] = TView(sb["PB"], 0, HDN * 1024, pmax=32)
    sb["l_k"] = TView(sb["PB"], 4096, HDN * 1024, pmax=32)
    sb["l_w"] = TView(sb["PB"], 8192, HDN * 256)
    sb["l_q"] = TView(sb["PB"], 9216, HDN * 256)
    sb["l_at"] = TView(sb["PB"], 10240, HDN * 128, pmax=32)
    sb["o4"] = TView(sb["PB"], 10752, 4 * 1024, pmax=32)
    sb["oT"] = TView(sb["PB"], 14848, CBN * ST)
    sb["wo_p"] = TView(sb["wpA"], 0, 8192)
    sb["b_q"] = TView(sb["PB"], 0, CBN * 512)
    sb["b_kr"] = TView(sb["PB"], 4096, CBN * 768)
    sb["b_vr"] = TView(sb["PB"], 10240, 6 * 1024)
    sb["b_ao"] = TView(sb["PB"], 16384, CBN * 512)
    sb["b_xn"] = TView(sb["PB"], 0, CBN * 512)

    for i in range(6):
        psum(f"p{i}", [128, 512], F32)
    psum("pb0", [128, 512], BF16)
    psum("pb1", [128, 512], BF16)

    stack_a.close()
    if phase_b:
        sbuf("wpB", [128, 24576], BF16)
        sbuf("BF1", [128, 1544], F32)

        sbuf("BT", [128, 2048], BF16)
        sbuf("msk", [128, 8 * 512], BF16)
        sbuf("prm", [128, 360], F32)
        sbuf("hstash", [128, 16], BF16)
        sbuf("lnb_b", [128, 8], BF16)
        sb["pcc"] = TView(sb["PB"], 4096, 4096)
        sb["agstg"] = TView(sb["PB"], 0, 2112)
        sb["b_at"] = TView(sb["PB"], 0, 1024)
        sb["dmp"] = TView(sb["PB"], 20480, 512)
        sb["zt2"] = TView(sb["BF1"], 0, 1024)
    sems = {}
    with ExitStack() as sem_stack:
        for n_ in ("qw", "qx", "qs", "ql", "qz", "qy", "cc", "qt", "qa",
                   "pe", "dve", "act", "pool"):
            sems[n_] = sem_stack.enter_context(nc.semaphore(n_))

        def gen_factory():
            ctr = {k: Ctr(v) for k, v in sems.items()}
            ctx = dict(banks=Banks())
            return _program(nc, ap, sb, ps, ctr, ctx, phase_b)

        with nc.Block() as block:
            @block.sync
            def _(e):
                for name, fn in gen_factory():
                    if name == "sync":
                        fn(e)
            @block.scalar
            def _(e):
                for name, fn in gen_factory():
                    if name == "act":
                        fn(e)
            @block.tensor
            def _(e):
                for name, fn in gen_factory():
                    if name == "pe":
                        fn(e)
            @block.vector
            def _(e):
                for name, fn in gen_factory():
                    if name == "dve":
                        fn(e)
            @block.gpsimd
            def _(e):
                for name, fn in gen_factory():
                    if name == "pool":
                        fn(e)
    stack.close()
    return nc


def _program(nc, ap, sb, ps, ctr, ctx, phase_b):
    yield from _phase_a(nc, ap, sb, ps, ctr, ctx)
    if phase_b:
        yield from _phase_b_all(nc, ap, sb, ps, ctr, ctx)
    yield _w("pool", ctr["qy"])
    yield _w("pool", ctr["qz"])



def _ln_stats(sb, ps, ctr, bk, src_fn, n_tok, pA, pB):
    pe, dve, act = ctr["pe"], ctr["dve"], ctr["act"]
    out = []
    for w in bk.wait(pA) + bk.wait(pB):
        out.append(w)
    for cb in range(CBN):
        sqb = sb["sq"] if cb % 2 == 0 else sb["sq2"]
        out.append(_w("act", pe, pe.v))
        out.append(_op("act", "activation", sqb[:, 0:n_tok], src_fn(cb),
                       AF.Square, inc=act))
        v_s = act.v
        out.append(_w("pe", act, v_s))
        out.append(_op("pe", "matmul", ps[pA][0:1, 0:n_tok], sb["c_onesc"][:, :],
                       src_fn(cb), start=(cb == 0), stop=(cb == CBN - 1), inc=pe))
        out.append(_op("pe", "matmul", ps[pB][0:1, 0:n_tok], sb["c_onesc"][:, :],
                       sqb[:, 0:n_tok], start=(cb == 0), stop=(cb == CBN - 1), inc=pe))
    v_stats = pe.v
    out.append(_w("dve", pe, v_stats))
    out.append(_op("dve", "tensor_scalar", sb["rowm"][0:1, 0:n_tok],
                   ps[pA][0:1, 0:n_tok], 1.0 / 1024.0, None, ALU.mult, inc=dve))
    out.append(_op("dve", "tensor_mul", sb["row1"][0:1, 0:n_tok],
                   sb["rowm"][0:1, 0:n_tok], sb["rowm"][0:1, 0:n_tok], inc=dve))
    out.append(_op("dve", "scalar_tensor_tensor", sb["row1"][0:1, 0:n_tok],
                   ps[pB][0:1, 0:n_tok], 1.0 / 1024.0, sb["row1"][0:1, 0:n_tok],
                   ALU.mult, ALU.subtract, inc=dve))
    v_var = dve.v
    out.append(_w("act", dve, v_var))
    out.append(_op("act", "activation", sb["row1"][0:1, ST:ST + n_tok],
                   sb["row1"][0:1, 0:n_tok], AF.Sqrt, bias=sb["cv"][0:1, 0:1], inc=act))
    v_sd = act.v
    out.append(_w("dve", act, v_sd))
    out.append(_op("dve", "reciprocal", sb["rowm"][0:1, ST:ST + n_tok],
                   sb["row1"][0:1, ST:ST + n_tok], inc=dve))
    out.append(_op("dve", "tensor_copy", sb["sq"][0:1, 0:n_tok],
                   sb["rowm"][0:1, 0:n_tok], inc=dve))
    out.append(_op("dve", "tensor_copy", sb["sq2"][0:1, 0:n_tok],
                   sb["rowm"][0:1, ST:ST + n_tok], inc=dve))
    v_rows = dve.v
    out.append(_w("pe", dve, v_rows))
    out.append(_op("pe", "matmul", ps[pA][:, 0:n_tok], sb["c_ones"][:, :],
                   sb["sq"][0:1, 0:n_tok], start=True, stop=True, inc=pe))
    out.append(_op("pe", "matmul", ps[pB][:, 0:n_tok], sb["c_ones"][:, :],
                   sb["sq2"][0:1, 0:n_tok], start=True, stop=True, inc=pe))
    return out, pe.v


def _phase_a(nc, ap, sb, ps, ctr, ctx):
    qw, qx, qs_, ql, qz, qy = (ctr[k] for k in ("qw", "qx", "qs", "ql", "qz", "qy"))
    pe, dve, act, pool, cc = ctr["pe"], ctr["dve"], ctr["act"], ctr["pool"], ctr["cc"]
    bk = ctx["banks"]

    for dst, src in (("c_idm", "idm"), ("c_slm", "slm"), ("c_ilm", "ilm"),
                     ("c_gsel", "gsel"), ("c_ones", "ones1"), ("c_onesc", "ones_c"),
                     ("c_esel", "esel"), ("c_rsel", "rsel"), ("c_prot", "prot"),
                     ("c_pinv", "pinv"), ("c_idmf", "idmf"), ("c_onesf", "ones1f"),
                     ("fracw_s", "fracw"), ("mixw_s", "mixw"), ("onw_s", "onw"),
                     ("s0m_s", "s0m"), ("hm_s", "hm"), ("fgb_s", "fgb"),
                     ("rl_s", "rotlog"), ("cv", "cvals")):
        yield _dma("sync", sb[dst][:, :], ap[src][:, :], qx)
    yield _dma("sync", sb["convs"][:, 0:32], ap["convq"][:, :], qx)
    yield _dma("sync", sb["convs"][:, 32:64], ap["convk"][:, :], qx)
    yield _dma("sync", sb["convs"][:, 64:96], ap["convv"][:, :], qx)
    yield _dma("pool", sb["fgw_b"][:, :], ap["fgw"][:, :], ctr["qt"])
    yield _dma("pool", sb["wbg_b"][:, :],
               ap["wbg"].rearrange("(kc p) n -> p kc n", p=128), ctr["qt"])
    v_const = qx.v

    yield _w("act", qx, v_const)
    yield _w("act", ctr["qt"], ctr["qt"].v)
    yield _op("act", "activation", sb["rl_s"][:, :], sb["rl_s"][:, :], AF.Sigmoid, inc=act)
    yield _op("act", "activation", sb["rl_b"][:, :], sb["rl_s"][:, :], AF.Copy, inc=act)
    v_mx = act.v
    yield _w("pe", act, v_mx)
    yield _op("pe", "matmul", ps["p0"][:, 0:4], sb["c_ones"][:, :], sb["rl_b"][:, :],
              start=True, stop=True, inc=pe)
    v_mmx = pe.v
    yield _w("dve", pe, v_mmx)
    yield _op("dve", "tensor_copy", sb["mixc"][:, :], ps["p0"][:, 0:4], inc=dve)
    yield _op("dve", "tensor_scalar", sb["imixc"][:, :], ps["p0"][:, 0:4], -1.0, 1.0,
              ALU.mult, ALU.add, inc=dve)
    bk.done("p0", dve)
    yield _op("dve", "memset", sb["S"][:, :], 0.0, inc=dve)
    yield _op("dve", "memset", sb["S_bf"][:, :], 0.0, inc=dve)
    yield _op("dve", "memset", sb["z_bf"][:, :], 0.0, inc=dve)
    yield _op("dve", "memset", sb["qpre"][:, :], 0.0, inc=dve)
    v_init = dve.v

    for st in range(NST):
        yield from _a_supertile(nc, ap, sb, ps, ctr, ctx, st, v_init)

    yield _w("sync", dve)
    yield _dma("sync", ap["ag_s_i"][:, :], sb["S"][:, :], qs_)
    if DBG:
        yield _dma("sync", ap["dbg_s"][:, :], sb["S"][:, :], qs_)
    yield _w("pool", qs_)
    cc.inc(1)
    def cc_fn(e, a=ap, h=cc.h):
        e.collective_compute(
            "AllGather", ALU.bypass,
            replica_groups=[[0, 1], [2, 3], [4, 5], [6, 7]],
            ins=[a["ag_s_i"].tensor.ap().opt()],
            outs=[a["ag_s_o"].tensor.ap().opt()],
        ).then_inc(h, 1)
    yield ("pool", cc_fn)
    yield _w("sync", cc)
    yield _dma("sync", sb["agt"][:, :], ap["ag_s_o"][0:128, :], ql)
    yield _w("dve", ql)
    yield _op("dve", "tensor_scalar", sb["S"][:, :], sb["agt"][:, :],
              sb["s0m_s"][:, 0:1], None, ALU.mult, inc=dve)
    yield _w("dve", dve, dve.v)
    yield _op("dve", "tensor_copy", sb["S_bf"][:, :], sb["S"][:, :], inc=dve)
    if DBG:
        yield _w("sync", dve, dve.v)
        yield _dma("sync", ap["dbg_s2"][:, :], sb["S"][:, :], qs_)
        yield _dma("sync", ap["dbg_ago"][:, :], ap["ag_s_o"][:, :], qs_)

    for st in range(NST):
        yield from _a_pass2_st(nc, ap, sb, ps, ctr, ctx, st)


def _a_supertile(nc, ap, sb, ps, ctr, ctx, st, v_init):
    qw, qx, qs_ = ctr["qw"], ctr["qx"], ctr["qs"]
    pe, dve, act, pool = ctr["pe"], ctr["dve"], ctr["act"], ctr["pool"]
    bk = ctx["banks"]
    t0 = st * ST

    yield _dma("sync", sb["sct"][:, 0:ST], ap["sdn"][:, t0:t0 + ST], qx)
    yield _dma("sync", sb["sct"][:, ST:2 * ST], ap["cdn"][:, t0:t0 + ST], qx)
    v_sct = qx.v
    if st == 0:
        yield _w("dve", dve, v_init)
    parts = [(0, 264), (264, 256)] if st == 0 else [(8, 256), (264, 256)]
    for (w0, wN) in parts:
        yield _w("sync", dve, dve.v)
        for cb in range(CBN):
            yield _dma("sync", sb["xf"][:, cb * 264: cb * 264 + wN],
                       ap["x_cm"][cb, :, t0 + w0: t0 + w0 + wN], qx)
        v_x = qx.v
        yield _w("dve", qx, v_x)
        for cb in range(CBN):
            yield _op("dve", "tensor_copy",
                      sb["xat"][:, cb * XW + w0: cb * XW + w0 + wN],
                      sb["xf"][:, cb * 264: cb * 264 + wN], inc=dve)
        v_cast = dve.v
        yield _w("act", dve, v_cast)
        ops, v_bc = _ln_stats(sb, ps, ctr, bk,
                              lambda cb, w0=w0, wN=wN: sb["xat"][:, cb * XW + w0: cb * XW + w0 + wN],
                              wN, "p2", "p3")
        for o_ in ops:
            yield o_
        yield _w("dve", pe, v_bc)
        for cb in range(CBN):
            yield _op("dve", "tensor_sub", sb["xf"][:, cb * 264: cb * 264 + wN],
                      sb["xf"][:, cb * 264: cb * 264 + wN], ps["p2"][:, 0:wN], inc=dve)
            yield _op("dve", "tensor_mul", sb["xat"][:, cb * XW + w0: cb * XW + w0 + wN],
                      sb["xf"][:, cb * 264: cb * 264 + wN], ps["p3"][:, 0:wN], inc=dve)
        bk.done("p2", dve)
        bk.done("p3", dve)
    v_xa = dve.v
    yield _w("sync", dve, v_xa)
    x0 = 0 if st == 0 else HALO_A + t0
    wsp = XW if st == 0 else ST
    osp = 0 if st == 0 else 8
    for cb in range(CBN):
        yield _dma("sync", ap["xa_d"][cb, :, x0:x0 + wsp],
                   sb["xat"][:, cb * XW + osp: cb * XW + osp + wsp], qs_)
    v_xa_spill = qs_.v

    hats = ("qh", "kh", "vh")
    evq = {0: None, 1: None}
    for p_i in range(3):
        woff = 0
        hat = sb[hats[p_i]]
        cvoff = p_i * 32
        yield _w("pool", pe, pe.v)
        yield _dma("pool", sb["wpA"][:, 0:8192],
                   ap[("wq", "wk", "wv")[p_i]].rearrange("(kc p) n -> p kc n", p=128), qw)
        yield _w("pe", qw, qw.v)
        yield _w("pe", dve, v_xa)
        yield _w("act", ctr["qs"], ctr["qs"].v)
        yield _w("act", dve, dve.v)
        ev = {}
        for mc in range(CBN):
            bname = "p4" if mc % 2 == 0 else "p5"
            pst = ps[bname]
            if evq[mc % 2] is not None:
                yield _w("pe", act, evq[mc % 2])
            for kc in range(CBN):
                yield _op("pe", "matmul", pst[:, 0:ST],
                          sb["wpA"][:, woff + kc * 1024 + mc * 128: woff + kc * 1024 + mc * 128 + 128],
                          sb["xat"][:, kc * XW + 8: kc * XW + 8 + ST],
                          start=(kc == 0), stop=(kc == CBN - 1), inc=pe)
            v_mm = pe.v
            yield _w("act", pe, v_mm)
            yield _op("act", "copy", sb["qpre"][:, mc * QW + 3: mc * QW + 3 + ST],
                      pst[:, 0:ST], inc=act)
            evq[mc % 2] = act.v
            ev[mc] = act.v
        vcarry = {}
        if st == 0:
            for mc in range(CBN):
                bname = "p4" if mc % 2 == 0 else "p5"
                pst = ps[bname]
                yield _w("pe", act, evq[mc % 2])
                for kc in range(CBN):
                    yield _op("pe", "matmul", pst[:, 0:3],
                              sb["wpA"][:, woff + kc * 1024 + mc * 128: woff + kc * 1024 + mc * 128 + 128],
                              sb["xat"][:, kc * XW + 5: kc * XW + 8],
                              start=(kc == 0), stop=(kc == CBN - 1), inc=pe)
                v_cmm = pe.v
                yield _w("act", pe, v_cmm)
                yield _op("act", "copy", sb["qpre"][:, mc * QW: mc * QW + 3],
                          pst[:, 0:3], inc=act)
                evq[mc % 2] = act.v
                vcarry[mc] = act.v
        else:
            yield _w("dve", act, act.v)
            for mc in range(CBN):
                yield _op("dve", "tensor_copy", sb["qpre"][:, mc * QW: mc * QW + 3],
                          sb["cr3"][:, p_i * 24 + mc * 3: p_i * 24 + mc * 3 + 3], inc=dve)
        vsilu = {}
        for mc in range(CBN):
            q0 = mc * QW + 3
            c0 = cvoff + mc * 4
            sc0 = (mc % 2) * ST
            yield _w("dve", act, ev[mc])
            if mc in vcarry:
                yield _w("dve", act, vcarry[mc])
            if mc >= 2:
                yield _w("dve", act, vsilu[mc - 2])
            yield _op("dve", "tensor_scalar", sb["rsc"][:, sc0:sc0 + ST],
                      sb["qpre"][:, q0 - 3: q0 - 3 + ST], sb["convs"][:, c0:c0 + 1],
                      None, ALU.mult, inc=dve)
            yield _op("dve", "scalar_tensor_tensor", sb["rsc"][:, sc0:sc0 + ST],
                      sb["qpre"][:, q0 - 2: q0 - 2 + ST], sb["convs"][:, c0 + 1:c0 + 2],
                      sb["rsc"][:, sc0:sc0 + ST], ALU.mult, ALU.add, inc=dve)
            yield _op("dve", "scalar_tensor_tensor", sb["rsc"][:, sc0:sc0 + ST],
                      sb["qpre"][:, q0 - 1: q0 - 1 + ST], sb["convs"][:, c0 + 2:c0 + 3],
                      sb["rsc"][:, sc0:sc0 + ST], ALU.mult, ALU.add, inc=dve)
            yield _op("dve", "scalar_tensor_tensor", sb["rsc"][:, sc0:sc0 + ST],
                      sb["qpre"][:, q0: q0 + ST], sb["convs"][:, c0 + 3:c0 + 4],
                      sb["rsc"][:, sc0:sc0 + ST], ALU.mult, ALU.add, inc=dve)
            v_y = dve.v
            yield _w("act", dve, v_y)
            yield _op("act", "activation", hat[:, mc * ST:(mc + 1) * ST],
                      sb["rsc"][:, sc0:sc0 + ST], AF.Silu, inc=act)
            vsilu[mc] = act.v
        v_hat = act.v
        yield _w("dve", act, v_hat)
        for mc in range(CBN):
            yield _op("dve", "tensor_copy",
                      sb["cr3"][:, p_i * 24 + mc * 3: p_i * 24 + mc * 3 + 3],
                      sb["qpre"][:, mc * QW + ST: mc * QW + ST + 3], inc=dve)
    v_hats = act.v

    yield _w("dve", qs_, v_xa_spill)
    for cb in range(CBN):
        yield _op("dve", "tensor_copy", sb["xat"][:, cb * XW: cb * XW + 8],
                  sb["xat"][:, cb * XW + ST: cb * XW + ST + 8], inc=dve)

    prev_bg = None
    prev_gc = None
    for gl in range(4):
        g = st * 4 + gl
        for w_ in bk.wait("p0"):
            yield w_
        if prev_bg is not None:
            yield _w("pe", act, prev_bg)
        yield _w("pe", dve, v_xa)
        for kc in range(CBN):
            yield _op("pe", "matmul", ps["p0"][0:128, 0:8],
                      sb["xat"][:, kc * XW + 8 + gl * GRP: kc * XW + 8 + gl * GRP + 128],
                      sb["wbg_b"][:, kc * 8:(kc + 1) * 8],
                      start=(kc == 0), stop=(kc == CBN - 1), inc=pe)
        v_bg = pe.v
        yield _w("act", pe, v_bg)
        yield _op("act", "activation", sb["bgf"][:, gl * 8:(gl + 1) * 8],
                  ps["p0"][0:128, 0:8], AF.Sigmoid, inc=act)
        yield _op("act", "activation", sb["bgb"][:, gl * 8:(gl + 1) * 8],
                  ps["p0"][0:128, 0:8], AF.Sigmoid, inc=act)
        prev_bg = act.v
        bk.done("p0", act)
        for w_ in bk.wait("p1"):
            yield w_
        yield _w("pe", act, prev_bg)
        if prev_gc is not None:
            yield _w("pe", dve, prev_gc)
        for h in range(HDN):
            yield _op("pe", "matmul", ps["p1"][0:1, h * 4:(h + 1) * 4],
                      sb["bgb"][:, gl * 8 + 4 + h: gl * 8 + 5 + h], sb["c_gsel"][:, :],
                      start=True, stop=True, inc=pe)
        v_gm = pe.v
        yield _w("dve", pe, v_gm)
        yield _op("dve", "tensor_copy", sb["sq"][0:1, 0:16], ps["p1"][0:1, 0:16], inc=dve)
        v_gmb = dve.v
        yield _w("pe", dve, v_gmb)
        yield _op("pe", "matmul", ps["p1"][:, 32:48], sb["c_ones"][:, :],
                  sb["sq"][0:1, 0:16], start=True, stop=True, inc=pe)
        v_gcb = pe.v
        yield _w("dve", pe, v_gcb)
        yield _op("dve", "tensor_copy", sb["gcol"][:, g * 16:(g + 1) * 16],
                  ps["p1"][:, 32:48], inc=dve)
        prev_gc = dve.v
        bk.done("p1", dve)

    yield _w("dve", qx, v_sct)
    for h in range(HDN):
        e_off = (2 * h) * ST
        o_off = (2 * h + 1) * ST
        for hat in (sb["qh"], sb["kh"]):
            yield _w("dve", act, v_hats)
            yield _w("pool", act, v_hats)
            yield _w("pool", dve, dve.v)
            yield _op("dve", "tensor_mul", sb["rsc"][:, 0:ST],
                      hat[:, e_off:e_off + ST], sb["sct"][:, ST:2 * ST], inc=dve)
            yield _op("dve", "tensor_mul", sb["rsc"][:, ST:2 * ST],
                      hat[:, o_off:o_off + ST], sb["sct"][:, 0:ST], inc=dve)
            yield _op("pool", "tensor_mul", sb["rsc"][:, 2 * ST:3 * ST],
                      hat[:, e_off:e_off + ST], sb["sct"][:, 0:ST], inc=pool)
            yield _op("pool", "tensor_mul", sb["rsc"][:, 3 * ST:4 * ST],
                      hat[:, o_off:o_off + ST], sb["sct"][:, ST:2 * ST], inc=pool)
            v_p4 = pool.v
            yield _w("dve", pool, v_p4)
            yield _op("dve", "tensor_sub", sb["rsc"][:, 0:ST],
                      sb["rsc"][:, 0:ST], sb["rsc"][:, ST:2 * ST], inc=dve)
            yield _op("dve", "tensor_add", sb["rsc"][:, 2 * ST:3 * ST],
                      sb["rsc"][:, 2 * ST:3 * ST], sb["rsc"][:, 3 * ST:4 * ST], inc=dve)
            v_re = dve.v
            for r in range(2):
                pn = "p0" if r == 0 else "p1"
                for w_ in bk.wait(pn):
                    yield w_
                yield _w("pe", act, v_hats)
                yield _op("pe", "matmul", ps[pn][:, 0:ST],
                          sb["c_pinv"][:, (0 * 2 + r) * 128: (0 * 2 + r) * 128 + 128],
                          hat[:, e_off:e_off + ST], start=True, stop=False, inc=pe)
                yield _op("pe", "matmul", ps[pn][:, 0:ST],
                          sb["c_pinv"][:, (1 * 2 + r) * 128: (1 * 2 + r) * 128 + 128],
                          hat[:, o_off:o_off + ST], start=False, stop=True, inc=pe)
            v_pmm = pe.v
            yield _w("dve", pe, v_pmm)
            yield _op("dve", "tensor_scalar", sb["rsc"][:, ST:2 * ST],
                      ps["p0"][:, 0:ST], sb["imixc"][:, h:h + 1], None, ALU.mult, inc=dve)
            yield _op("dve", "scalar_tensor_tensor", hat[:, e_off:e_off + ST],
                      sb["rsc"][:, 0:ST], sb["mixc"][:, h:h + 1],
                      sb["rsc"][:, ST:2 * ST], ALU.mult, ALU.add, inc=dve)
            bk.done("p0", dve)
            yield _op("dve", "tensor_scalar", sb["rsc"][:, 3 * ST:4 * ST],
                      ps["p1"][:, 0:ST], sb["imixc"][:, h:h + 1], None, ALU.mult, inc=dve)
            yield _op("dve", "scalar_tensor_tensor", hat[:, o_off:o_off + ST],
                      sb["rsc"][:, 2 * ST:3 * ST], sb["mixc"][:, h:h + 1],
                      sb["rsc"][:, 3 * ST:4 * ST], ALU.mult, ALU.add, inc=dve)
            bk.done("p1", dve)
            v_rot = dve.v
            yield _w("act", dve, v_rot)
            yield _w("act", pe, pe.v)
            yield _op("act", "activation", sb["sq"][:, 0:ST],
                      hat[:, e_off:e_off + ST], AF.Square, inc=act)
            v_s1 = act.v
            yield _op("act", "activation", sb["sq2"][:, 0:ST],
                      hat[:, o_off:o_off + ST], AF.Square, inc=act)
            v_s2 = act.v
            for w_ in bk.wait("p0"):
                yield w_
            yield _w("pe", act, v_s1)
            yield _op("pe", "matmul", ps["p0"][0:1, 0:ST], sb["c_onesc"][:, :],
                      sb["sq"][:, 0:ST], start=True, stop=False, inc=pe)
            yield _w("pe", act, v_s2)
            yield _op("pe", "matmul", ps["p0"][0:1, 0:ST], sb["c_onesc"][:, :],
                      sb["sq2"][:, 0:ST], start=False, stop=True, inc=pe)
            v_nm = pe.v
            yield _w("act", pe, v_nm)
            yield _op("act", "activation", sb["nrow"][0:1, 0:ST],
                      ps["p0"][0:1, 0:ST], AF.Sqrt, bias=sb["cv"][0:1, 1:2], inc=act)
            v_sqr = act.v
            yield _w("dve", act, v_sqr)
            yield _op("dve", "reciprocal", sb["nrow"][0:1, 0:ST],
                      sb["nrow"][0:1, 0:ST], inc=dve)
            yield _op("dve", "tensor_copy", sb["sq"][0:1, 0:ST],
                      sb["nrow"][0:1, 0:ST], inc=dve)
            v_nr = dve.v
            yield _w("pe", dve, v_nr)
            yield _op("pe", "matmul", ps["p0"][:, 0:ST], sb["c_ones"][:, :],
                      sb["sq"][0:1, 0:ST], start=True, stop=True, inc=pe)
            v_bn = pe.v
            yield _w("dve", pe, v_bn)
            yield _op("dve", "tensor_mul", hat[:, e_off:e_off + ST],
                      hat[:, e_off:e_off + ST], ps["p0"][:, 0:ST], inc=dve)
            yield _op("dve", "tensor_mul", hat[:, o_off:o_off + ST],
                      hat[:, o_off:o_off + ST], ps["p0"][:, 0:ST], inc=dve)
            bk.done("p0", dve)
    v_qk = dve.v

    for gl in range(4):
        g = st * 4 + gl
        yield from _dn_group(nc, ap, sb, ps, ctr, ctx, st, gl, g, v_qk)


def _dn_group(nc, ap, sb, ps, ctr, ctx, st, gl, g, v_qk):
    pe, dve, act, pool = ctr["pe"], ctr["dve"], ctr["act"], ctr["pool"]
    qs_, q3 = ctr["qs"], ctr["qa"]
    bk = ctx["banks"]
    qh, kh, vh = sb["qh"], sb["kh"], sb["vh"]
    tok0 = gl * GRP

    def hsl(hat, h, dc, off=0, w=GRP):
        c0 = (2 * h + dc) * ST + tok0 + off
        return hat[:, c0:c0 + w]

    yield _w("dve", qs_, qs_.v)
    for h in range(HDN):
        pn = "p0" if h % 2 == 0 else "p1"
        for w_ in bk.wait(pn):
            yield w_
        yield _w("pe", dve, v_qk)
        for dc in range(DKC):
            yield _op("pe", "matmul", ps[pn][:, 0:128],
                      hsl(kh, h, dc), hsl(kh, h, dc),
                      start=(dc == 0), stop=(dc == 1), inc=pe)
        v_g = pe.v
        yield _w("dve", pe, v_g)
        yield _op("dve", "scalar_tensor_tensor", sb["Lt"][:, h * 128:(h + 1) * 128],
                  ps[pn][:, 0:128], sb["bgf"][:, gl * 8 + h: gl * 8 + h + 1],
                  sb["c_slm"][:, :], ALU.mult, ALU.mult, inc=dve)
        bk.done(pn, dve)
        yield _op("dve", "transpose", sb["LtT"][:, h * 128:(h + 1) * 128],
                  sb["Lt"][:, h * 128:(h + 1) * 128], inc=dve)
    v_l = dve.v

    for h in range(HDN):
        for dc in range(DKC):
            i = h * 2 + dc
            pbn = "pb0" if i % 2 == 0 else "pb1"
            pb = ps[pbn]
            for w_ in bk.wait(pbn):
                yield w_
            yield _op("pe", "transpose", pb[:, 0:128], hsl(kh, h, dc),
                      sb["c_idm"][:, :], inc=pe)
            yield _op("pe", "transpose", pb[:, 128:256], hsl(vh, h, dc),
                      sb["c_idm"][:, :], inc=pe)
            v_tr = pe.v
            yield _w("dve", pe, v_tr)
            yield _op("dve", "tensor_copy",
                      sb["kT"][:, h * 256 + dc * 128: h * 256 + dc * 128 + 128],
                      pb[:, 0:128], inc=dve)
            yield _op("dve", "tensor_scalar",
                      sb["tb"][:, h * 512 + 256 + dc * 128: h * 512 + 256 + dc * 128 + 128],
                      pb[:, 0:128], sb["bgf"][:, gl * 8 + h: gl * 8 + h + 1],
                      None, ALU.mult, inc=dve)
            yield _op("dve", "tensor_scalar",
                      sb["tb"][:, h * 512 + dc * 128: h * 512 + dc * 128 + 128],
                      pb[:, 128:256], sb["bgf"][:, gl * 8 + h: gl * 8 + h + 1],
                      None, ALU.mult, inc=dve)
            bk.done(pbn, dve)
    v_tinit = dve.v

    chv = {h: v_tinit for h in range(HDN)}
    sqv = {h: v_l for h in range(HDN)}
    for step in range(5):
        sign = ALU.subtract if step == 0 else ALU.add
        for h in range(HDN):
            pn = "p2" if h % 2 == 0 else "p3"
            for w_ in bk.wait(pn):
                yield w_
            yield _w("pe", dve, chv[h])
            yield _w("pe", dve, sqv[h])
            yield _op("pe", "matmul", ps[pn][:, 0:512],
                      sb["LtT"][:, h * 128:(h + 1) * 128],
                      sb["tb"][:, h * 512:(h + 1) * 512],
                      start=True, stop=True, inc=pe)
            v_c = pe.v
            yield _w("dve", pe, v_c)
            yield _op("dve", "tensor_tensor", sb["tb"][:, h * 512:(h + 1) * 512],
                      sb["tb"][:, h * 512:(h + 1) * 512], ps[pn][:, 0:512],
                      sign, inc=dve)
            chv[h] = dve.v
            bk.done(pn, dve)
        if step < 4:
            for h in range(HDN):
                pn = "p0" if h % 2 == 0 else "p1"
                for w_ in bk.wait(pn):
                    yield w_
                yield _w("pe", dve, sqv[h])
                yield _op("pe", "matmul", ps[pn][:, 0:128],
                          sb["LtT"][:, h * 128:(h + 1) * 128],
                          sb["Lt"][:, h * 128:(h + 1) * 128],
                          start=True, stop=True, inc=pe)
                v_sq = pe.v
                yield _w("dve", pe, v_sq)
                yield _w("dve", dve, chv[h])
                yield _op("dve", "tensor_copy", sb["Lt"][:, h * 128:(h + 1) * 128],
                          ps[pn][:, 0:128], inc=dve)
                yield _op("dve", "transpose", sb["LtT"][:, h * 128:(h + 1) * 128],
                          sb["Lt"][:, h * 128:(h + 1) * 128], inc=dve)
                sqv[h] = dve.v
                bk.done(pn, dve)
    v_chain = dve.v

    wtv = {}
    for h in range(HDN):
        for dc in range(DKC):
            i = h * 2 + dc
            pbn = "pb0" if i % 2 == 0 else "pb1"
            pb = ps[pbn]
            for w_ in bk.wait(pbn):
                yield w_
            yield _w("pe", dve, chv[h])
            yield _op("pe", "transpose", pb[:, 256:384],
                      sb["tb"][:, h * 512 + 256 + dc * 128: h * 512 + 256 + dc * 128 + 128],
                      sb["c_idm"][:, :], inc=pe)
            v_tw = pe.v
            yield _w("dve", pe, v_tw)
            yield _op("dve", "tensor_copy",
                      sb["wT"][:, h * 256 + dc * 128: h * 256 + dc * 128 + 128],
                      pb[:, 256:384], inc=dve)
            bk.done(pbn, dve)
            wtv[i] = dve.v
    v_wt = dve.v

    atv = {}
    for h in range(HDN):
        pn = "p0" if h % 2 == 0 else "p1"
        for w_ in bk.wait(pn):
            yield w_
        yield _w("pe", dve, v_qk)
        for c in range(NCH):
            for dc in range(DKC):
                yield _op("pe", "matmul", ps[pn][0:32, c * 32: c * 32 + 32],
                          hsl(kh, h, dc, off=c * 32, w=32),
                          hsl(qh, h, dc, off=c * 32, w=32),
                          start=(dc == 0), stop=(dc == 1), inc=pe)
        v_at = pe.v
        yield _w("dve", pe, v_at)
        for c in range(NCH):
            yield _op("dve", "tensor_tensor",
                      sb["atT"][:, h * 128 + c * 32: h * 128 + c * 32 + 32],
                      ps[pn][0:32, c * 32: c * 32 + 32], sb["c_ilm"][:, :],
                      ALU.mult, inc=dve)
        bk.done(pn, dve)
        atv[h] = dve.v
    v_att = dve.v

    yield _w("sync", dve, v_att)
    for h in range(HDN):
        yield _dma("sync", ap["sp_u"][g, h], sb["tb"][:, h * 512: h * 512 + 256], qs_)
        yield _dma("sync", ap["sp_w"][g, h], sb["wT"][:, h * 256:(h + 1) * 256], qs_)
        yield _dma("sync", ap["sp_k"][g, h], sb["kT"][:, h * 256:(h + 1) * 256], qs_)
        yield _dma("sync", ap["sp_at"][g, h], sb["atT"][:, h * 128:(h + 1) * 128], qs_)
        yield _dma("sync", ap["sp_q"][g, h, :, 0:128], hsl(qh, h, 0), qs_)
        yield _dma("sync", ap["sp_q"][g, h, :, 128:256], hsl(qh, h, 1), qs_)
    yield _w("act", dve, v_att)
    for h in range(HDN):
        yield _dma("act", sb["u3"][:, h * 256:(h + 1) * 256],
                   sb["tb"][96:128, h * 512: h * 512 + 256], q3)
        yield _dma("act", sb["k3"][:, h * 256:(h + 1) * 256],
                   sb["kT"][96:128, h * 256:(h + 1) * 256], q3)
    v_fix = q3.v

    sv = {h: v_att for h in range(HDN)}
    for c in range(NCH):
        p0c = c * 32 if c < 3 else 0
        for h in range(HDN):
            pw = "p2" if h % 2 == 0 else "p3"
            pd = "p4" if h % 2 == 0 else "p5"
            for w_ in bk.wait(pw):
                yield w_
            yield _w("pe", dve, sv[h])
            if c == 3:
                yield _w("pe", q3, v_fix)
            for dc in range(DKC):
                yield _op("pe", "matmul", ps[pw][p0c:p0c + 32, 0:256],
                          sb["wT"][:, h * 256 + dc * 128 + c * 32: h * 256 + dc * 128 + c * 32 + 32],
                          sb["S_bf"][:, h * 512 + dc * 256: h * 512 + dc * 256 + 256],
                          start=(dc == 0), stop=(dc == 1), inc=pe,
                          tile_position=(0, p0c))
            v_ws = pe.v
            yield _w("dve", pe, v_ws)
            if c < 3:
                usrc = sb["tb"][c * 32: c * 32 + 32, h * 512: h * 512 + 256]
            else:
                usrc = sb["u3"][:, h * 256:(h + 1) * 256]
            yield _op("dve", "tensor_sub", sb["utb"][p0c:p0c + 32, h * 256:(h + 1) * 256],
                      usrc, ps[pw][p0c:p0c + 32, 0:256], inc=dve)
            v_ut = dve.v
            bk.done(pw, dve)
            for w_ in bk.wait(pd):
                yield w_
            yield _w("pe", dve, v_ut)
            for dc in range(DKC):
                if c < 3:
                    ksrc = sb["kT"][c * 32: c * 32 + 32, h * 256 + dc * 128: h * 256 + dc * 128 + 128]
                else:
                    ksrc = sb["k3"][:, h * 256 + dc * 128: h * 256 + dc * 128 + 128]
                yield _op("pe", "matmul", ps[pd][:, dc * 256:(dc + 1) * 256],
                          ksrc, sb["utb"][p0c:p0c + 32, h * 256:(h + 1) * 256],
                          start=True, stop=True, inc=pe,
                          tile_position=(p0c, 0))
            v_d = pe.v
            yield _w("dve", pe, v_d)
            yield _op("dve", "scalar_tensor_tensor", sb["S"][:, h * 512:(h + 1) * 512],
                      sb["S"][:, h * 512:(h + 1) * 512],
                      sb["gcol"][:, g * 16 + h * 4 + c: g * 16 + h * 4 + c + 1],
                      ps[pd][:, 0:512], ALU.mult, ALU.add, inc=dve)
            yield _w("dve", dve, dve.v)
            yield _op("dve", "tensor_copy", sb["S_bf"][:, h * 512:(h + 1) * 512],
                      sb["S"][:, h * 512:(h + 1) * 512], inc=dve)
            sv[h] = dve.v
            bk.done(pd, dve)


def _a_pass2_st(nc, ap, sb, ps, ctr, ctx, st):
    pe, dve, act, pool = ctr["pe"], ctr["dve"], ctr["act"], ctr["pool"]
    ql, qs_, qx, qy, qz, qw = (ctr[k] for k in ("ql", "qs", "qx", "qy", "qz", "qw"))
    bk = ctx["banks"]
    t0 = st * ST

    v_wo = {}
    yield _w("pool", pe, pe.v)
    for half in range(2):
        for kc in range(CBN):
            yield _dma("pool", sb["wo_p"][:, half * 4096 + kc * 512: half * 4096 + kc * 512 + 512],
                       ap["wo_dn"][kc * 128:(kc + 1) * 128, half * 512:(half + 1) * 512], qw)
        v_wo[half] = qw.v

    yield _w("sync", pe, pe.v)
    yield _w("sync", dve, dve.v)
    for cb in range(CBN):
        yield _dma("sync", sb["xat"][:, cb * XW: cb * XW + XW],
                   ap["xa_d"][cb, :, t0: t0 + XW], qx)
    v_xa = qx.v

    for gl in range(4):
        g = st * 4 + gl
        yield _w("sync", pe, pe.v)
        yield _w("sync", dve, dve.v)
        for h in range(HDN):
            hb = h
            yield _dma("sync", sb["l_u"][:, hb * 1024: hb * 1024 + 1024],
                       ap["sp_u"][g, h].rearrange("(c t) d -> t c d", t=32), ql)
            yield _dma("sync", sb["l_k"][:, hb * 1024: hb * 1024 + 1024],
                       ap["sp_k"][g, h].rearrange("(c t) d -> t c d", t=32), ql)
            yield _dma("sync", sb["l_w"][:, hb * 256: hb * 256 + 256],
                       ap["sp_w"][g, h], ql)
            yield _dma("sync", sb["l_q"][:, hb * 256: hb * 256 + 256],
                       ap["sp_q"][g, h], ql)
            yield _dma("sync", sb["l_at"][:, hb * 128: hb * 128 + 128],
                       ap["sp_at"][g, h], ql)
        v_ld = ql.v

        sv = {h: dve.v for h in range(HDN)}
        for c in range(NCH):
            for h in range(HDN):
                hb = h
                pw = "p2" if h % 2 == 0 else "p3"
                pd = "p4" if h % 2 == 0 else "p5"
                po = "p0" if h % 2 == 0 else "p1"
                for w_ in bk.wait(pw):
                    yield w_
                yield _w("pe", dve, sv[h])
                yield _w("pe", ql, v_ld)
                for dc in range(DKC):
                    yield _op("pe", "matmul", ps[pw][0:32, 0:256],
                              sb["l_w"][:, hb * 256 + dc * 128 + c * 32: hb * 256 + dc * 128 + c * 32 + 32],
                              sb["S_bf"][:, h * 512 + dc * 256: h * 512 + dc * 256 + 256],
                              start=(dc == 0), stop=(dc == 1), inc=pe)
                v_ws = pe.v
                yield _w("dve", pe, v_ws)
                yield _op("dve", "tensor_sub", sb["utb"][0:32, h * 256:(h + 1) * 256],
                          sb["l_u"][:, hb * 1024 + c * 256: hb * 1024 + c * 256 + 256],
                          ps[pw][0:32, 0:256], inc=dve)
                v_ut = dve.v
                bk.done(pw, dve)
                for w_ in bk.wait(po) + bk.wait(pd):
                    yield w_
                yield _w("pe", dve, v_ut)
                for dc in range(DKC):
                    yield _op("pe", "matmul", ps[po][0:32, 0:256],
                              sb["l_q"][:, hb * 256 + dc * 128 + c * 32: hb * 256 + dc * 128 + c * 32 + 32],
                              sb["S_bf"][:, h * 512 + dc * 256: h * 512 + dc * 256 + 256],
                              start=(dc == 0), stop=False, inc=pe)
                yield _op("pe", "matmul", ps[po][0:32, 0:256],
                          sb["l_at"][:, hb * 128 + c * 32: hb * 128 + c * 32 + 32],
                          sb["utb"][0:32, h * 256:(h + 1) * 256],
                          start=False, stop=True, inc=pe)
                for dc in range(DKC):
                    yield _op("pe", "matmul", ps[pd][:, dc * 256:(dc + 1) * 256],
                              sb["l_k"][:, hb * 1024 + c * 256 + dc * 128: hb * 1024 + c * 256 + dc * 128 + 128],
                              sb["utb"][0:32, h * 256:(h + 1) * 256],
                              start=True, stop=True, inc=pe)
                v_all = pe.v
                yield _w("dve", pe, v_all)
                yield _op("dve", "tensor_copy",
                          sb["o4"][:, h * 1024 + c * 256: h * 1024 + c * 256 + 256],
                          ps[po][0:32, 0:256], inc=dve)
                yield _op("dve", "scalar_tensor_tensor", sb["S"][:, h * 512:(h + 1) * 512],
                          sb["S"][:, h * 512:(h + 1) * 512],
                          sb["gcol"][:, g * 16 + h * 4 + c: g * 16 + h * 4 + c + 1],
                          ps[pd][:, 0:512], ALU.mult, ALU.add, inc=dve)
                yield _w("dve", dve, dve.v)
                yield _op("dve", "tensor_copy", sb["S_bf"][:, h * 512:(h + 1) * 512],
                          sb["S"][:, h * 512:(h + 1) * 512], inc=dve)
                sv[h] = dve.v
                bk.done(po, dve)
                bk.done(pd, dve)

        for h in range(HDN):
            yield _w("act", dve, sv[h])
            yield _w("act", pe, pe.v)
            for c in range(NCH):
                yield _op("act", "activation", sb["osq"][:, 0:256],
                          sb["o4"][:, h * 1024 + c * 256: h * 1024 + c * 256 + 256],
                          AF.Square, accum_out=sb["orms"][:, h * 4 + c: h * 4 + c + 1],
                          inc=act)
            v_sq = act.v
            yield _op("act", "activation", sb["orms"][:, h * 4: h * 4 + 4],
                      sb["orms"][:, h * 4: h * 4 + 4], AF.Sqrt,
                      scale=1.0 / 256.0, bias=sb["cv"][0:32, 0:1], inc=act)
            v_rs = act.v
            yield _w("dve", act, v_rs)
            yield _op("dve", "reciprocal", sb["orms"][:, h * 4: h * 4 + 4],
                      sb["orms"][:, h * 4: h * 4 + 4], inc=dve)
            yield _w("dve", dve, dve.v)
            for c in range(NCH):
                yield _op("dve", "tensor_scalar",
                          sb["o4"][:, h * 1024 + c * 256: h * 1024 + c * 256 + 256],
                          sb["o4"][:, h * 1024 + c * 256: h * 1024 + c * 256 + 256],
                          sb["orms"][:, h * 4 + c: h * 4 + c + 1], None, ALU.mult,
                          inc=dve)
            v_os = dve.v
            if DBG:
                yield _w("sync", dve, v_os)
                yield _dma("sync", ap["dbg_o"][g, h], sb["o4"][:, h * 1024:(h + 1) * 1024], ctr["qs"])
            for c in range(NCH):
                for dc in range(DKC):
                    i = (c * 2 + dc) % 2
                    pbn = "pb0" if i == 0 else "pb1"
                    pb = ps[pbn]
                    for w_ in bk.wait(pbn):
                        yield w_
                    yield _w("pe", dve, v_os)
                    yield _op("pe", "transpose", pb[:, 384:416],
                              sb["o4"][:, h * 1024 + c * 256 + dc * 128: h * 1024 + c * 256 + dc * 128 + 128],
                              sb["c_idm"][0:32, 0:32], inc=pe)
                    v_tp = pe.v
                    yield _w("act", pe, v_tp)
                    yield _op("act", "activation",
                              sb["oT"][:, (2 * h + dc) * ST + gl * 128 + c * 32:
                                       (2 * h + dc) * ST + gl * 128 + c * 32 + 32],
                              pb[:, 384:416], AF.Copy,
                              scale=sb["onw_s"][:, dc:dc + 1], inc=act)
                    bk.done(pbn, act)
    v_oT = act.v

    if st == 0:
        yield _w("dve", qx, v_xa)
        for cb in range(CBN):
            yield _op("dve", "tensor_add", sb["small"][:, cb:cb + 1],
                      sb["fracw_s"][:, cb * 8 + 4: cb * 8 + 5],
                      sb["fracw_s"][:, cb * 8 + 5: cb * 8 + 6], inc=dve)
            yield _w("dve", dve, dve.v)
            yield _op("dve", "tensor_add", sb["small"][:, cb:cb + 1],
                      sb["small"][:, cb:cb + 1],
                      sb["fracw_s"][:, cb * 8 + 6: cb * 8 + 7], inc=dve)
            yield _w("dve", dve, dve.v)
            yield _op("dve", "tensor_add", sb["small"][:, cb:cb + 1],
                      sb["small"][:, cb:cb + 1],
                      sb["fracw_s"][:, cb * 8 + 7: cb * 8 + 8], inc=dve)
            yield _w("dve", dve, dve.v)
            yield _op("dve", "tensor_scalar", sb["small"][:, cb:cb + 1],
                      sb["small"][:, cb:cb + 1], 2.0, None, ALU.add, inc=dve)
            yield _w("dve", dve, dve.v)

    if DBG and st == 0:
        yield _w("sync", dve, dve.v)
        yield _dma("sync", ap["dbg_sm"][:, 0:16], sb["small"][:, :], ctr["qs"])
        yield _dma("sync", ap["dbg_sm"][:, 16:80], sb["fracw_s"][:, :], ctr["qs"])
    yield _w("dve", qx, v_xa)
    for w_ in bk.wait("p2"):
        yield w_
    for cb in range(CBN):
        xa0 = cb * XW + 8
        yield _op("dve", "tensor_scalar", sb["fr"][:, cb * ST:(cb + 1) * ST],
                  sb["xat"][:, xa0: xa0 + ST], sb["small"][:, cb:cb + 1],
                  None, ALU.mult, inc=dve)
        for i in range(4):
            dil = 2 ** i
            yield _op("dve", "scalar_tensor_tensor", sb["fr"][:, cb * ST:(cb + 1) * ST],
                      sb["xat"][:, xa0 - dil: xa0 - dil + ST],
                      sb["fracw_s"][:, cb * 8 + i: cb * 8 + i + 1],
                      sb["fr"][:, cb * ST:(cb + 1) * ST], ALU.mult, ALU.add, inc=dve)
        v_fcb = dve.v
        yield _w("act", dve, v_fcb)
        yield _w("act", pe, pe.v)
        yield _op("act", "activation", sb["sq"][:, 0:ST],
                  sb["fr"][:, cb * ST:(cb + 1) * ST], AF.Square, inc=act)
        v_sf = act.v
        yield _w("pe", act, v_sf)
        yield _op("pe", "matmul", ps["p2"][0:1, 0:ST], sb["c_onesc"][:, :],
                  sb["sq"][:, 0:ST], start=(cb == 0), stop=(cb == CBN - 1), inc=pe)
    v_frs = pe.v
    yield _w("act", pe, v_frs)
    yield _op("act", "activation", sb["nrow"][0:1, 0:ST], ps["p2"][0:1, 0:ST],
              AF.Sqrt, scale=1.0 / 1024.0, bias=sb["cv"][0:1, 0:1], inc=act)
    v_fsd = act.v
    bk.done("p2", act)
    yield _w("dve", act, v_fsd)
    yield _op("dve", "reciprocal", sb["nrow"][0:1, 0:ST], sb["nrow"][0:1, 0:ST], inc=dve)
    yield _op("dve", "tensor_copy", sb["sq"][0:1, 0:ST], sb["nrow"][0:1, 0:ST], inc=dve)
    v_fr2 = dve.v
    for w_ in bk.wait("p3"):
        yield w_
    yield _w("pe", dve, v_fr2)
    yield _op("pe", "matmul", ps["p3"][:, 0:ST], sb["c_ones"][:, :],
              sb["sq"][0:1, 0:ST], start=True, stop=True, inc=pe)
    v_fbc = pe.v
    yield _w("dve", pe, v_fbc)
    for cb in range(CBN):
        yield _op("dve", "scalar_tensor_tensor", sb["fr"][:, cb * ST:(cb + 1) * ST],
                  sb["fr"][:, cb * ST:(cb + 1) * ST], sb["mixw_s"][:, cb:cb + 1],
                  ps["p3"][:, 0:ST], ALU.mult, ALU.mult, inc=dve)
    v_frn = dve.v
    bk.done("p3", dve)
    if DBG:
        yield _w("sync", dve, v_frn)
        yield _dma("sync", ap["dbg_fr"][st], sb["fr"][:, 0:CBN * 512], ctr["qs"])

    for w_ in bk.wait("p1"):
        yield w_
    yield _w("pe", qx, v_xa)
    for kc in range(CBN):
        yield _op("pe", "matmul", ps["p1"][0:1, 0:ST],
                  sb["fgw_b"][:, kc:kc + 1],
                  sb["xat"][:, kc * XW + 8: kc * XW + 8 + ST],
                  start=(kc == 0), stop=(kc == CBN - 1), inc=pe)
    v_gt = pe.v
    yield _w("act", pe, v_gt)
    yield _op("act", "activation", sb["sq"][0:1, 0:ST], ps["p1"][0:1, 0:ST],
              AF.Sigmoid, bias=sb["fgb_s"][0:1, 0:1], inc=act)
    v_gs = act.v
    bk.done("p1", act)
    if DBG:
        yield _w("sync", act, v_gs)
        yield _dma("sync", ap["dbg_gt"][st], sb["sq"][0:1, 0:ST], ctr["qs"])
    for w_ in bk.wait("p3"):
        yield w_
    yield _w("pe", act, v_gs)
    yield _op("pe", "matmul", ps["p3"][:, 0:ST], sb["c_ones"][:, :],
              sb["sq"][0:1, 0:ST], start=True, stop=True, inc=pe)
    v_gbc = pe.v

    yield _w("dve", qz, qz.v)
    x1v = {}
    xv = {}
    for mc in range(CBN):
        if mc >= 4:
            yield _w("sync", dve, x1v[mc - 4])
        elif mc == 0:
            yield _w("sync", dve, dve.v)
        yield _dma("sync", sb["xf"][:, (mc % 4) * 512: (mc % 4) * 512 + ST],
                   ap["x_cm"][mc, :, HALO_A + t0: HALO_A + t0 + ST], qx)
        xv[mc] = qx.v
        half, j = mc // 4, mc % 4
        pn = "p4" if mc % 2 == 0 else "p5"
        for w_ in bk.wait(pn):
            yield w_
        yield _w("pe", qw, v_wo[half])
        yield _w("pe", act, v_oT)
        for kc in range(CBN):
            yield _op("pe", "matmul", ps[pn][:, 0:ST],
                      sb["wo_p"][:, half * 4096 + kc * 512 + j * 128: half * 4096 + kc * 512 + j * 128 + 128],
                      sb["oT"][:, kc * ST:(kc + 1) * ST],
                      start=(kc == 0), stop=(kc == CBN - 1), inc=pe)
        v_wom = pe.v
        yield _w("dve", pe, v_wom)
        yield _w("dve", qx, xv[mc])
        if mc == 0:
            yield _w("dve", pe, v_gbc)
            yield _w("dve", dve, v_frn)
        yield _op("dve", "tensor_add", sb["x1t"][:, mc * ST:(mc + 1) * ST],
                  sb["xf"][:, (mc % 4) * 512: (mc % 4) * 512 + ST], ps[pn][:, 0:ST], inc=dve)
        yield _op("dve", "tensor_mul", sb["fr"][:, mc * ST:(mc + 1) * ST],
                  sb["fr"][:, mc * ST:(mc + 1) * ST], ps["p3"][:, 0:ST], inc=dve)
        yield _op("dve", "tensor_add", sb["x1t"][:, mc * ST:(mc + 1) * ST],
                  sb["x1t"][:, mc * ST:(mc + 1) * ST],
                  sb["fr"][:, mc * ST:(mc + 1) * ST], inc=dve)
        x1v[mc] = dve.v
        bk.done(pn, dve)
    v_x1 = dve.v
    bk.done("p3", dve)

    yield _w("sync", dve, v_x1)
    for cb in range(CBN):
        yield _dma("sync", ap["y_cm"][cb, :, t0:t0 + ST],
                   sb["x1t"][:, cb * ST:(cb + 1) * ST], qy)
    if st == NST - 1:
        for cb in range(CBN):
            yield _dma("sync", ap["ag_x_i"][:, cb * 258:(cb + 1) * 258],
                       sb["x1t"][:, cb * ST + ST - 258: (cb + 1) * ST], qs_)
    v_ywr = qy.v
    v_tail = qs_.v

    yield _w("dve", qy, v_ywr)
    yield _w("dve", qs_, v_tail)
    for cb in range(CBN):
        sqb = sb["sq"] if cb % 2 == 0 else sb["sq2"]
        yield _w("dve", pe, pe.v)
        yield _op("dve", "tensor_copy", sqb[:, 0:ST],
                  sb["x1t"][:, cb * ST:(cb + 1) * ST], inc=dve)
        v_c = dve.v
        yield _w("act", dve, v_c)
        sqo = sb["sq2"] if cb % 2 == 0 else sb["sq"]
        yield _op("act", "activation", sqo[:, 0:ST], sqb[:, 0:ST], AF.Square, inc=act)
        v_s = act.v
        if cb == 0:
            for w_ in bk.wait("p0") + bk.wait("p1"):
                yield w_
        yield _w("pe", act, v_s)
        yield _op("pe", "matmul", ps["p0"][0:1, 0:ST], sb["c_onesc"][:, :],
                  sqb[:, 0:ST], start=(cb == 0), stop=(cb == CBN - 1), inc=pe)
        yield _op("pe", "matmul", ps["p1"][0:1, 0:ST], sb["c_onesc"][:, :],
                  sqo[:, 0:ST], start=(cb == 0), stop=(cb == CBN - 1), inc=pe)
    v_zs = pe.v
    yield _w("dve", pe, v_zs)
    yield _op("dve", "tensor_scalar", sb["rowm"][0:1, 0:ST], ps["p0"][0:1, 0:ST],
              1.0 / 1024.0, None, ALU.mult, inc=dve)
    yield _op("dve", "tensor_mul", sb["row1"][0:1, 0:ST],
              sb["rowm"][0:1, 0:ST], sb["rowm"][0:1, 0:ST], inc=dve)
    yield _op("dve", "scalar_tensor_tensor", sb["row1"][0:1, 0:ST],
              ps["p1"][0:1, 0:ST], 1.0 / 1024.0, sb["row1"][0:1, 0:ST],
              ALU.mult, ALU.subtract, inc=dve)
    v_zv = dve.v
    yield _w("act", dve, v_zv)
    yield _op("act", "activation", sb["row1"][0:1, ST:2 * ST], sb["row1"][0:1, 0:ST],
              AF.Sqrt, bias=sb["cv"][0:1, 0:1], inc=act)
    v_zsd = act.v
    yield _w("dve", act, v_zsd)
    yield _op("dve", "reciprocal", sb["rowm"][0:1, ST:2 * ST],
              sb["row1"][0:1, ST:2 * ST], inc=dve)
    yield _op("dve", "tensor_copy", sb["sq"][0:1, 0:ST], sb["rowm"][0:1, 0:ST], inc=dve)
    yield _op("dve", "tensor_copy", sb["sq2"][0:1, 0:ST], sb["rowm"][0:1, ST:2 * ST], inc=dve)
    v_zr = dve.v
    yield _w("pe", dve, v_zr)
    yield _op("pe", "matmul", ps["p0"][:, 0:ST], sb["c_ones"][:, :],
              sb["sq"][0:1, 0:ST], start=True, stop=True, inc=pe)
    yield _op("pe", "matmul", ps["p1"][:, 0:ST], sb["c_ones"][:, :],
              sb["sq2"][0:1, 0:ST], start=True, stop=True, inc=pe)
    v_zbc = pe.v
    yield _w("dve", pe, v_zbc)
    for cb in range(CBN):
        yield _op("dve", "tensor_sub", sb["x1t"][:, cb * ST:(cb + 1) * ST],
                  sb["x1t"][:, cb * ST:(cb + 1) * ST], ps["p0"][:, 0:ST], inc=dve)
        yield _w("dve", dve, dve.v)
        yield _op("dve", "tensor_mul", sb["x1t"][:, cb * ST:(cb + 1) * ST],
                  sb["x1t"][:, cb * ST:(cb + 1) * ST], ps["p1"][:, 0:ST], inc=dve)
        yield _w("dve", dve, dve.v)
        yield _op("dve", "tensor_copy",
                  sb["z_bf"][:, cb * ZW + HALO_F + t0: cb * ZW + HALO_F + t0 + ST],
                  sb["x1t"][:, cb * ST:(cb + 1) * ST], inc=dve)
    v_z0 = dve.v
    bk.done("p0", dve)
    bk.done("p1", dve)
    yield _w("sync", dve, v_z0)
    for cb in range(CBN):
        yield _dma("sync", ap["zf32"][cb, :, t0:t0 + ST],
                   sb["x1t"][:, cb * ST:(cb + 1) * ST], ctr["qz"])

    if st == NST - 1:
        yield from _x1_tail_ag(nc, ap, sb, ps, ctr, ctx)


def _x1_tail_ag(nc, ap, sb, ps, ctr, ctx):
    pe, dve, act, cc = ctr["pe"], ctr["dve"], ctr["act"], ctr["cc"]
    ql, qs_ = ctr["ql"], ctr["qs"]
    bk = ctx["banks"]
    yield _w("pool", qs_, qs_.v)
    cc.inc(1)
    def cc_fn(e, a=ap, h=cc.h):
        e.collective_compute(
            "AllGather", ALU.bypass,
            replica_groups=[[0, 1], [2, 3], [4, 5], [6, 7]],
            ins=[a["ag_x_i"].tensor.ap().opt()],
            outs=[a["ag_x_o"].tensor.ap().opt()],
        ).then_inc(h, 1)
    yield ("pool", cc_fn)
    yield _w("sync", cc)
    yield _w("sync", ctr["qz"], ctr["qz"].v)
    yield _dma("sync", sb["x1t"][:, 0:CBN * 258], ap["ag_x_o"][0:128, :], ql)
    v_tl = ql.v
    yield _w("dve", ql, v_tl)
    W = 258
    for cb in range(CBN):
        yield _op("dve", "tensor_scalar", sb["x1t"][:, cb * W:(cb + 1) * W],
                  sb["x1t"][:, cb * W:(cb + 1) * W], sb["hm_s"][:, 0:1],
                  None, ALU.mult, inc=dve)
    v_mk = dve.v
    for cb in range(CBN):
        sqb = sb["sq"] if cb % 2 == 0 else sb["sq2"]
        sqo = sb["sq2"] if cb % 2 == 0 else sb["sq"]
        yield _w("dve", pe, pe.v)
        yield _op("dve", "tensor_copy", sqb[:, 0:W],
                  sb["x1t"][:, cb * W:(cb + 1) * W], inc=dve)
        v_c = dve.v
        yield _w("act", dve, v_c)
        yield _op("act", "activation", sqo[:, 0:W], sqb[:, 0:W], AF.Square, inc=act)
        v_s = act.v
        if cb == 0:
            for w_ in bk.wait("p0") + bk.wait("p1"):
                yield w_
        yield _w("pe", act, v_s)
        yield _op("pe", "matmul", ps["p0"][0:1, 0:W], sb["c_onesc"][:, :],
                  sqb[:, 0:W], start=(cb == 0), stop=(cb == CBN - 1), inc=pe)
        yield _op("pe", "matmul", ps["p1"][0:1, 0:W], sb["c_onesc"][:, :],
                  sqo[:, 0:W], start=(cb == 0), stop=(cb == CBN - 1), inc=pe)
    v_hs = pe.v
    yield _w("dve", pe, v_hs)
    yield _op("dve", "tensor_scalar", sb["rowm"][0:1, 0:W], ps["p0"][0:1, 0:W],
              1.0 / 1024.0, None, ALU.mult, inc=dve)
    yield _op("dve", "tensor_mul", sb["row1"][0:1, 0:W],
              sb["rowm"][0:1, 0:W], sb["rowm"][0:1, 0:W], inc=dve)
    yield _op("dve", "scalar_tensor_tensor", sb["row1"][0:1, 0:W],
              ps["p1"][0:1, 0:W], 1.0 / 1024.0, sb["row1"][0:1, 0:W],
              ALU.mult, ALU.subtract, inc=dve)
    v_hv = dve.v
    yield _w("act", dve, v_hv)
    yield _op("act", "activation", sb["row1"][0:1, ST:ST + W],
              sb["row1"][0:1, 0:W], AF.Sqrt, bias=sb["cv"][0:1, 0:1], inc=act)
    v_hsd = act.v
    yield _w("dve", act, v_hsd)
    yield _op("dve", "reciprocal", sb["rowm"][0:1, ST:ST + W],
              sb["row1"][0:1, ST:ST + W], inc=dve)
    yield _op("dve", "tensor_copy", sb["sq"][0:1, 0:W], sb["rowm"][0:1, 0:W], inc=dve)
    yield _op("dve", "tensor_copy", sb["sq2"][0:1, 0:W],
              sb["rowm"][0:1, ST:ST + W], inc=dve)
    v_hr = dve.v
    yield _w("pe", dve, v_hr)
    yield _op("pe", "matmul", ps["p0"][:, 0:W], sb["c_ones"][:, :],
              sb["sq"][0:1, 0:W], start=True, stop=True, inc=pe)
    yield _op("pe", "matmul", ps["p1"][:, 0:W], sb["c_ones"][:, :],
              sb["sq2"][0:1, 0:W], start=True, stop=True, inc=pe)
    v_hbc = pe.v
    yield _w("dve", pe, v_hbc)
    for cb in range(CBN):
        yield _op("dve", "tensor_sub", sb["x1t"][:, cb * W:(cb + 1) * W],
                  sb["x1t"][:, cb * W:(cb + 1) * W], ps["p0"][:, 0:W], inc=dve)
        yield _op("dve", "tensor_mul", sb["x1t"][:, cb * W:(cb + 1) * W],
                  sb["x1t"][:, cb * W:(cb + 1) * W], ps["p1"][:, 0:W], inc=dve)
        yield _op("dve", "tensor_copy",
                  sb["z_bf"][:, cb * ZW + 14: cb * ZW + 14 + W],
                  sb["x1t"][:, cb * W:(cb + 1) * W], inc=dve)
    bk.done("p0", dve)
    bk.done("p1", dve)



THIRDS = [list(range(0, 8)), list(range(8, 15)), list(range(15, 22))]


def _phase_b_all(nc, ap, sb, ps, ctr, ctx):
    pe, dve, act, pool = ctr["pe"], ctr["dve"], ctr["act"], ctr["pool"]
    qw, qx, qs_, ql, qz, qy, cc = (ctr[k] for k in ("qw", "qx", "qs", "ql", "qz", "qy", "cc"))
    bk = ctx["banks"]

    for c_ in (dve, pe, act, qy, qz, qs_, qx, ql):
        yield _w("pool", c_, c_.v)
    yield _op("pool", "memset", sb["hstash"][:, :], 0.0, inc=pool)
    v_bar = pool.v
    for e_ in ("dve", "pe", "act", "sync"):
        yield _w(e_, pool, v_bar)

    for nm, off in (("qsr", 0), ("ksr", 48), ("lng", 96), ("lnbc", 144), ("b2c", 192)):
        for l in range(NL):
            yield _dma("sync", sb["prm"][:, off + l * 8: off + l * 8 + 8],
                       ap[nm][l], qx)
    yield _w("pool", ctr["qt"], ctr["qt"].v)
    for s_ in range(2):
        for ih in range(2):
            yield _dma("pool", sb["msk"][:, (s_ * 2 + ih) * 512: (s_ * 2 + ih) * 512 + 512],
                       ap["fmask"][s_, ih * 128:(ih + 1) * 128, :], ctr["qt"])
    for ih2 in range(2):
        yield _dma("pool", sb["msk"][:, 2048 + ih2 * 1024: 2048 + ih2 * 1024 + 1024],
                   ap["fmask_t"][:, ih2 * 1024:(ih2 + 1) * 1024], ctr["qt"])
    v_prm = qx.v
    yield _w("dve", qx, v_prm)
    yield _w("pe", qx, v_prm)

    if DBGZ:
        yield _w("sync", ctr["qz"], ctr["qz"].v)
        yield _dma("sync", ap["dbg_z"][2 * NL - 1], ap["zf32"][:, :, :], ctr["qz"])
        yield _w("sync", ctr["dve"], ctr["dve"].v)
        yield _w("sync", ctr["qx"], ctr["qx"].v)
        for cb in range(CBN):
            yield _dma("sync", ap["dbg_zh"][:, cb * 280: cb * 280 + 280],
                       sb["z_bf"][:, cb * ZW: cb * ZW + 280], ctr["qz"])
    for l in range(NL):
        yield from _b_attn(nc, ap, sb, ps, ctr, ctx, l, v_prm)
        if DBGZ:
            yield _w("sync", ctr["qz"], ctr["qz"].v)
            yield _dma("sync", ap["dbg_z"][2 * l], ap["zf32"][:, :, :], ctr["qz"])
        yield from _b_ffn(nc, ap, sb, ps, ctr, ctx, l)
        if DBGZ and l < NL - 1:
            yield _w("sync", ctr["qz"], ctr["qz"].v)
            yield _dma("sync", ap["dbg_z"][2 * l + 1], ap["zf32"][:, :, :], ctr["qz"])


def _qk_pipe(nc, sb, ps, ctr, ctx, loc, N, tb0, sc_col, dbg=None):
    pe, dve, act = ctr["pe"], ctr["dve"], ctr["act"]
    bk = ctx["banks"]
    for w_ in bk.wait("p2"):
        yield w_
    for mc in range(CBN):
        buf, col = loc(mc)
        sqb = sb["sq"] if mc % 2 == 0 else sb["sq2"]
        yield _w("act", pe, pe.v)
        yield _w("act", dve, dve.v)
        yield _op("act", "activation", sqb[:, 0:N], buf[:, col: col + N],
                  AF.Square, inc=act)
        v_s = act.v
        yield _w("pe", act, v_s)
        yield _op("pe", "matmul", ps["p2"][0:16, 0:N],
                  sb["c_esel"][:, mc * 16: mc * 16 + 16], sqb[:, 0:N],
                  start=(mc == 0), stop=(mc == CBN - 1), inc=pe)
    v_nm = pe.v
    if dbg is not None:
        yield _w("act", pe, v_nm)
        yield _op("act", "copy", sb["BF1"][0:16, 512:512 + N], ps["p2"][0:16, 0:N],
                  inc=act)
        yield _w("sync", act, act.v)
        yield _dma("sync", dbg[0][0:16, 256:256 + N], sb["BF1"][0:16, 512:512 + N],
                   dbg[1])
    yield _w("act", pe, v_nm)
    yield _op("act", "activation", sb["BF1"][0:16, 0:N], ps["p2"][0:16, 0:N],
              AF.Sqrt, inc=act)
    v_sq = act.v
    if dbg is not None:
        yield _w("sync", act, v_sq)
        yield _dma("sync", dbg[0][0:16, 0:N], sb["BF1"][0:16, 0:N], dbg[1])
        yield _w("pe", dbg[1], dbg[1].v)
    bk.done("p2", act)
    yield _w("dve", act, v_sq)
    yield _op("dve", "tensor_scalar", sb["BF1"][0:16, 0:N], sb["BF1"][0:16, 0:N],
              1e-12, None, ALU.max, inc=dve)
    yield _w("dve", dve, dve.v)
    yield _op("dve", "reciprocal", sb["BF1"][0:16, 0:N], sb["BF1"][0:16, 0:N], inc=dve)
    yield _w("dve", dve, dve.v)
    yield _op("dve", "tensor_copy", sb["sq"][0:16, 0:N], sb["BF1"][0:16, 0:N], inc=dve)
    v_rq = dve.v
    for mc in range(CBN):
        buf, col = loc(mc)
        pn = "p3" if mc % 2 == 0 else "p2"
        for w_ in bk.wait(pn):
            yield w_
        yield _w("pe", dve, v_rq)
        yield _op("pe", "matmul", ps[pn][:, 0:N],
                  sb["c_rsel"][:, mc * 128: mc * 128 + 128], sb["sq"][0:16, 0:N],
                  start=True, stop=True, inc=pe)
        v_b = pe.v
        yield _w("dve", pe, v_b)
        yield _op("dve", "scalar_tensor_tensor", buf[:, col: col + N],
                  buf[:, col: col + N], sb["prm"][:, sc_col + mc: sc_col + mc + 1],
                  ps[pn][:, 0:N], ALU.mult, ALU.mult, inc=dve)
        bk.done(pn, dve)
    for mc in range(CBN):
        buf, col = loc(mc)
        pn = "p3" if mc % 2 == 0 else "p2"
        for w_ in bk.wait(pn):
            yield w_
        yield _w("pe", dve, dve.v)
        yield _op("pe", "matmul", ps[pn][:, 0:N], sb["c_prot"][:, :],
                  buf[:, col: col + N], start=True, stop=True, inc=pe)
        v_p = pe.v
        yield _w("dve", pe, v_p)
        yield _w("dve", ctr["qt"], ctr["qt"].v)
        yield _op("dve", "tensor_mul", sb["BF1"][:, 0:N],
                  buf[:, col: col + N], sb["BT"][:, tb0: tb0 + N], inc=dve)
        yield _op("dve", "tensor_mul", sb["BF1"][:, 512:512 + N],
                  ps[pn][:, 0:N], sb["BT"][:, tb0 + 512: tb0 + 512 + N], inc=dve)
        yield _w("dve", dve, dve.v)
        yield _op("dve", "tensor_add", buf[:, col: col + N],
                  sb["BF1"][:, 0:N], sb["BF1"][:, 512:512 + N], inc=dve)
        bk.done(pn, dve)


def _b_attn(nc, ap, sb, ps, ctr, ctx, l, v_prm):
    pe, dve, act, pool = ctr["pe"], ctr["dve"], ctr["act"], ctr["pool"]
    qw, qx, qs_, ql, qz, qy, cc = (ctr[k] for k in ("qw", "qx", "qs", "ql", "qz", "qy", "cc"))
    bk = ctx["banks"]

    yield _w("pool", pe, pe.v)
    yield _w("pool", act, act.v)
    for kc in range(CBN):
        yield _dma("pool", sb["wpB"][:, kc * 3072: kc * 3072 + 3072],
                   ap["wqkv"][l, kc * 128:(kc + 1) * 128, :], qw)
    v_wqkv = qw.v
    yield _w("pool", qw, v_wqkv)
    for half in range(2):
        for kc in range(CBN):
            yield _dma("pool", sb["wo_p"][:, half * 4096 + kc * 512: half * 4096 + kc * 512 + 512],
                       ap["wo_ff"][l, kc * 128:(kc + 1) * 128, half * 512:(half + 1) * 512], qw)
    v_wo = qw.v
    yield _w("sync", dve, dve.v)
    yield _dma("sync", sb["prm"][:, 240:284], ap["b1c"][l], qx)
    yield _w("pool", pe, pe.v)
    yield _w("pool", ctr["qt"], ctr["qt"].v)
    yield _dma("pool", sb["lnb_b"][:, :], ap["lnbc"][l], ctr["qt"])

    if l > 0:
        yield _w("sync", cc, cc.v)
        yield _dma("sync", sb["agstg"][:, 0: CBN * 258], ap["ag_z_o"][0:128, :], ql)
        v_tl = ql.v
        yield _w("dve", ql, v_tl)
        for cb in range(CBN):
            yield _op("dve", "tensor_scalar",
                      sb["z_bf"][:, cb * ZW + 14: cb * ZW + 14 + 258],
                      sb["agstg"][:, cb * 258:(cb + 1) * 258],
                      sb["hm_s"][:, 0:1], None, ALU.mult, inc=dve)
    v_hz = dve.v

    def zsh(kc, t, w):
        off = -1 if kc >= 4 else 0
        return sb["z_bf"][:, kc * ZW + HALO_F + t + off: kc * ZW + HALO_F + t + off + w]

    yield _w("pool", dve, dve.v)
    yield _w("pool", ctr["qt"], ctr["qt"].v)
    for i_, nm in enumerate(("cs", "ss", "ci", "si")):
        yield _dma("pool", sb["BT"][:, i_ * 512: i_ * 512 + 256], ap[nm][:, 16:272],
                   ctr["qt"])
    v_tbl = ctr["qt"].v

    evq = {0: None, 1: None}
    for mc in range(CBN):
        pn = "p4" if mc % 2 == 0 else "p5"
        if evq[mc % 2] is not None:
            yield _w("pe", act, evq[mc % 2])
        yield _w("pe", qw, v_wqkv)
        yield _w("pe", dve, v_hz)
        for kc in range(CBN):
            yield _op("pe", "matmul", ps[pn][:, 0:256],
                      sb["wpB"][:, kc * 3072 + 1024 + mc * 128: kc * 3072 + 1024 + mc * 128 + 128],
                      zsh(kc, -256, 256), start=(kc == 0), stop=(kc == CBN - 1), inc=pe)
        v_mm = pe.v
        yield _w("act", pe, v_mm)
        yield _op("act", "copy", sb["b_kr"][:, mc * 768: mc * 768 + 256],
                  ps[pn][:, 0:256], inc=act)
        evq[mc % 2] = act.v
    yield _w("dve", act, act.v)
    yield _w("dve", ctr["qt"], v_tbl)
    if DBGZ and l == 0:
        yield _w("sync", act, act.v)
        for mc in range(CBN):
            yield _dma("sync", ap["dbg_kh"][:, mc * 256: mc * 256 + 256],
                       sb["b_kr"][:, mc * 768: mc * 768 + 256], qz)
        yield _w("dve", qz, qz.v)
        yield _w("act", qz, qz.v)
    yield from _qk_pipe(nc, sb, ps, ctr, ctx,
                        lambda mc: (sb["b_kr"], mc * 768), 256, 1024, 48 + l * 8,
                        dbg=((ap["dbg_nr"], qz) if (DBGZ and l == 0) else None))
    if DBGZ and l == 0:
        yield _w("sync", ctr["dve"], ctr["dve"].v)
        for mc in range(CBN):
            yield _dma("sync", ap["dbg_kh"][:, 2048 + mc * 256: 2048 + mc * 256 + 256],
                       sb["b_kr"][:, mc * 768: mc * 768 + 256], qz)
    v_hk = dve.v

    for tc in range(2):
        for vc in range(2):
            pn = "p4" if (tc * 2 + vc) % 2 == 0 else "p5"
            yield _w("pe", dve, dve.v)
            for kc in range(CBN):
                yield _op("pe", "matmul", ps[pn][:, 0:512],
                          zsh(kc, -256 + tc * 128, 128),
                          sb["wpB"][:, kc * 3072 + 2048 + vc * 512: kc * 3072 + 2048 + vc * 512 + 512],
                          start=(kc == 0), stop=(kc == CBN - 1), inc=pe)
            v_vm = pe.v
            yield _w("dve", pe, v_vm)
            yield _op("dve", "tensor_copy",
                      sb["b_vr"][:, tc * 1024 + vc * 512: tc * 1024 + vc * 512 + 512],
                      ps[pn][:, 0:512], inc=dve)
    v_hv = dve.v

    yield from _b_mini_attn(nc, ap, sb, ps, ctr, ctx, l, zsh, v_wqkv, v_wo)

    for T in range(8):
        yield from _b_attn_ttile(nc, ap, sb, ps, ctr, ctx, l, T, zsh,
                                 v_wqkv, v_wo)

    yield _w("pool", pe, pe.v)
    yield _w("pool", dve, dve.v)
    yield _w("pool", qz, qz.v)
    yield _w("pool", ctr["qt"], ctr["qt"].v)
    for cb in range(CBN):
        yield _dma("pool", sb["z_bf"][:, cb * ZW + HALO_F: cb * ZW + HALO_F + TOK],
                   ap["zf32"][cb], ctr["qt"])
    v_rf = ctr["qt"].v
    yield _w("dve", ctr["qt"], v_rf)
    for cb in range(CBN):
        yield _op("dve", "tensor_copy", sb["z_bf"][:, cb * ZW + 271: cb * ZW + 272],
                  sb["hstash"][:, cb: cb + 1], inc=dve)


def _b_attn_ttile(nc, ap, sb, ps, ctr, ctx, l, T, zsh, v_wqkv, v_wo):
    pe, dve, act, pool = ctr["pe"], ctr["dve"], ctr["act"], ctr["pool"]
    qx, qz = ctr["qx"], ctr["qz"]
    bk = ctx["banks"]
    t0 = T * 512

    yield _w("pool", dve, dve.v)
    yield _w("pool", ctr["qt"], ctr["qt"].v)
    for i_, nm in enumerate(("cs", "ss", "ci", "si")):
        yield _dma("pool", sb["BT"][:, i_ * 512: i_ * 512 + 512],
                   ap[nm][:, HALO_F + t0: HALO_F + t0 + 512], ctr["qt"])
    v_tbl = ctr["qt"].v

    if T > 0:
        yield _w("dve", pe, pe.v)
        for mc in range(CBN):
            yield _op("dve", "tensor_copy", sb["b_kr"][:, mc * 768: mc * 768 + 256],
                      sb["b_kr"][:, mc * 768 + 512: mc * 768 + 768], inc=dve)
    v_ringcp = dve.v

    evq = {0: None, 1: None}
    for mc in range(16):
        pn = "p4" if mc % 2 == 0 else "p5"
        if evq[mc % 2] is not None:
            yield _w("pe", act, evq[mc % 2])
        yield _w("pe", ctr["qw"], v_wqkv)
        yield _w("pe", dve, v_ringcp)
        for kc in range(CBN):
            yield _op("pe", "matmul", ps[pn][:, 0:512],
                      sb["wpB"][:, kc * 3072 + mc * 128: kc * 3072 + mc * 128 + 128],
                      zsh(kc, t0, 512), start=(kc == 0), stop=(kc == CBN - 1), inc=pe)
        v_mm = pe.v
        yield _w("act", pe, v_mm)
        if mc < 8:
            dst = sb["b_q"][:, mc * 512: mc * 512 + 512]
        else:
            dst = sb["b_kr"][:, (mc - 8) * 768 + 256: (mc - 8) * 768 + 768]
        yield _op("act", "copy", dst, ps[pn][:, 0:512], inc=act)
        evq[mc % 2] = act.v
    yield _w("dve", act, act.v)
    yield _w("dve", ctr["qt"], v_tbl)
    yield from _qk_pipe(nc, sb, ps, ctr, ctx,
                        lambda mc: (sb["b_q"], mc * 512), 512, 0, 0 + l * 8)
    yield from _qk_pipe(nc, sb, ps, ctr, ctx,
                        lambda mc: (sb["b_kr"], mc * 768 + 256), 512, 1024, 48 + l * 8)
    v_qk = dve.v
    if DBGZ and l == 0 and T == 0:
        yield _w("sync", dve, v_qk)
        yield _dma("sync", ap["dbg_q"][:, :], sb["b_q"][:, 0:4096], qz)
        yield _dma("sync", ap["dbg_k"][:, :], sb["b_kr"][:, 0:6144], qz)

    for tc in range(4):
        slot = (2 + T * 4 + tc) % 6
        for vc in range(2):
            pn = "p4" if (tc * 2 + vc) % 2 == 0 else "p5"
            yield _w("pe", dve, dve.v)
            for kc in range(CBN):
                yield _op("pe", "matmul", ps[pn][:, 0:512],
                          zsh(kc, t0 + tc * 128, 128),
                          sb["wpB"][:, kc * 3072 + 2048 + vc * 512: kc * 3072 + 2048 + vc * 512 + 512],
                          start=(kc == 0), stop=(kc == CBN - 1), inc=pe)
            v_vm = pe.v
            yield _w("dve", pe, v_vm)
            yield _op("dve", "tensor_copy",
                      sb["b_vr"][:, slot * 1024 + vc * 512: slot * 1024 + vc * 512 + 512],
                      ps[pn][:, 0:512], inc=dve)
    v_v = dve.v
    if DBGZ and l == 0 and T == 0:
        yield _w("sync", dve, v_v)
        yield _dma("sync", ap["dbg_v"][:, :], sb["b_vr"][:, 0:6144], qz)

    for wbl in range(2):
        wb = 2 * T + wbl
        mslot = 0 if wb == 0 else 1
        for h in range(HF):
            mc = h // 2
            hrow = (h % 2) * 64
            kwin = sb["b_kr"][hrow:hrow + 64, mc * 768 + wbl * 256: mc * 768 + wbl * 256 + 512]
            for ich in range(2):
                pn = "p0" if ich == 0 else "p1"
                for w_ in bk.wait(pn):
                    yield w_
                yield _w("pe", dve, v_v)
                yield _op("pe", "matmul", ps[pn][:, 0:512],
                          sb["b_q"][hrow:hrow + 64, mc * 512 + wbl * 256 + ich * 128: mc * 512 + wbl * 256 + ich * 128 + 128],
                          kwin, start=True, stop=True, inc=pe)
                v_s = pe.v
                yield _w("dve", pe, v_s)
                if ich == 1:
                    yield _w("dve", act, act.v)
                yield _op("dve", "tensor_add", sb["BF1"][:, 0:512],
                          ps[pn][:, 0:512],
                          sb["msk"][:, (mslot * 2 + ich) * 512: (mslot * 2 + ich) * 512 + 512],
                          inc=dve)
                bk.done(pn, dve)
                yield _w("dve", dve, dve.v)
                yield _op("dve", "tensor_reduce",
                          sb["BF1"][:, 1024 + ich: 1024 + ich + 1],
                          sb["BF1"][:, 0:512], mybir.AxisListType.X, ALU.max,
                          negate=True, inc=dve)
                v_m = dve.v
                yield _w("act", dve, v_m)
                yield _op("act", "activation", sb["dmp"][:, 0:512],
                          sb["BF1"][:, 0:512], AF.Exp,
                          bias=sb["BF1"][:, 1024 + ich: 1024 + ich + 1],
                          accum_out=sb["BF1"][:, 1028 + ich: 1028 + ich + 1], inc=act)
            v_p1 = act.v
            yield _w("dve", act, v_p1)
            yield _op("dve", "reciprocal", sb["BF1"][:, 1028:1030],
                      sb["BF1"][:, 1028:1030], inc=dve)
            yield _w("dve", dve, dve.v)
            yield from _bm_bz(nc, sb, ps, ctr, ctx)
            for jc in range(4):
                pn = "p2" if jc < 2 else "p3"
                sub = jc % 2
                for w_ in (bk.wait(pn) if (jc in (0, 2)) else []):
                    yield w_
                if jc == 0:
                    yield _w("pe", dve, dve.v)
                yield _op("pe", "matmul", ps[pn][:, sub * 256: sub * 256 + 256],
                          kwin[:, jc * 128: jc * 128 + 128],
                          sb["b_q"][hrow:hrow + 64, mc * 512 + wbl * 256: mc * 512 + wbl * 256 + 256],
                          start=True, stop=False, inc=pe)
                yield _op("pe", "matmul", ps[pn][:, sub * 256: sub * 256 + 256],
                          sb["c_idm"][:, :],
                          sb["msk"][:, 2048 + (mslot * 4 + jc) * 256: 2048 + (mslot * 4 + jc) * 256 + 256],
                          start=False, stop=True, inc=pe)
            v_t2 = pe.v
            yield _w("dve", pe, v_t2)
            for jc in range(4):
                pn = "p2" if jc < 2 else "p3"
                sub = jc % 2
                yield _op("dve", "tensor_add", ps[pn][:, sub * 256: sub * 256 + 256],
                          ps[pn][:, sub * 256: sub * 256 + 256],
                          sb["BF1"][:, 512:768], inc=dve)
            v_sub = dve.v
            yield _w("act", dve, v_sub)
            yield _op("act", "activation", sb["sq"][:, 0:512],
                      ps["p2"][:, 0:512], AF.Exp, inc=act)
            yield _op("act", "activation", sb["sq2"][:, 0:512],
                      ps["p3"][:, 0:512], AF.Exp, inc=act)
            v_at = act.v
            bk.done("p2", act)
            bk.done("p3", act)
            if DBGZ and l == 0 and T == 0 and wb == 0 and h == 0:
                yield _w("sync", act, v_at)
                yield _w("sync", dve, dve.v)
                yield _dma("sync", ap["dbg_at"][:, 0:512], sb["sq"][:, 0:512], qz)
                yield _dma("sync", ap["dbg_at"][:, 512:1024], sb["sq2"][:, 0:512], qz)
                yield _dma("sync", ap["dbg_sc"][:, 0:2], sb["BF1"][:, 1024:1026], qz)
                yield _dma("sync", ap["dbg_sc"][:, 2:4], sb["BF1"][:, 1028:1030], qz)
                yield _dma("sync", ap["dbg_sc"][:, 4:6], sb["BF1"][:, 512:514], qz)
                yield _dma("sync", ap["dbg_sc"][:, 6:8], sb["BF1"][:, 768:770], qz)
            for w_ in bk.wait("p4"):
                yield w_
            yield _w("pe", act, v_at)
            for jc in range(4):
                jtc = (2 * wb + jc) % 6
                atb = sb["sq"] if jc < 2 else sb["sq2"]
                yield _op("pe", "matmul", ps["p4"][hrow:hrow + 64, 0:256],
                          sb["b_vr"][:, jtc * 1024 + h * 64: jtc * 1024 + h * 64 + 64],
                          atb[:, (jc % 2) * 256: (jc % 2) * 256 + 256],
                          start=(jc == 0), stop=(jc == 3), inc=pe,
                          tile_position=(0, hrow))
            v_av = pe.v
            yield _w("dve", pe, v_av)
            yield _op("dve", "tensor_mul",
                      sb["b_ao"][hrow:hrow + 64, mc * 512 + wbl * 256: mc * 512 + wbl * 256 + 256],
                      ps["p4"][hrow:hrow + 64, 0:256],
                      sb["BF1"][hrow:hrow + 64, 768:1024], inc=dve)
            bk.done("p4", dve)
    v_attn = dve.v
    if DBGZ and l == 0 and T == 0:
        yield _w("sync", dve, v_attn)
        yield _dma("sync", ap["dbg_ao"][:, :], sb["b_ao"][:, 0:4096], qz)

    for mc in range(CBN):
        half, j = mc // 4, mc % 4
        pn = "p4" if mc % 2 == 0 else "p5"
        for w_ in bk.wait(pn):
            yield w_
        yield _w("pe", ctr["qw"], v_wo)
        yield _w("pe", dve, v_attn)
        for kc in range(CBN):
            yield _op("pe", "matmul", ps[pn][:, 0:512],
                      sb["wo_p"][:, half * 4096 + kc * 512 + j * 128: half * 4096 + kc * 512 + j * 128 + 128],
                      sb["b_ao"][:, kc * 512:(kc + 1) * 512],
                      start=(kc == 0), stop=(kc == CBN - 1), inc=pe)
        v_wm = pe.v
        yield _w("sync", qz, qz.v)
        if mc < 2:
            yield _w("sync", dve, v_attn)
            yield _w("sync", act, act.v)
        yield _dma("sync", sb["zt2"][:, (mc % 2) * 512: (mc % 2) * 512 + 512],
                   ap["zf32"][mc, :, t0:t0 + 512], qx)
        v_zl = qx.v
        yield _w("dve", pe, v_wm)
        yield _w("dve", qx, v_zl)
        yield _op("dve", "tensor_add", sb["zt2"][:, (mc % 2) * 512: (mc % 2) * 512 + 512],
                  sb["zt2"][:, (mc % 2) * 512: (mc % 2) * 512 + 512],
                  ps[pn][:, 0:512], inc=dve)
        v_zu = dve.v
        bk.done(pn, dve)
        yield _w("sync", dve, v_zu)
        yield _dma("sync", ap["zf32"][mc, :, t0:t0 + 512],
                   sb["zt2"][:, (mc % 2) * 512: (mc % 2) * 512 + 512], qz)


def _bm_bz(nc, sb, ps, ctr, ctx):
    pe, dve, act = ctr["pe"], ctr["dve"], ctr["act"]
    bk = ctx["banks"]
    for w_ in bk.wait("p5"):
        yield w_
    yield _w("pe", dve, dve.v)
    for ich in range(2):
        yield _op("pe", "transpose", ps["p5"][0:1, ich * 128: ich * 128 + 128],
                  sb["BF1"][:, 1024 + ich: 1024 + ich + 1], sb["c_idmf"][:, :],
                  inc=pe)
        yield _op("pe", "transpose", ps["p5"][0:1, 256 + ich * 128: 256 + ich * 128 + 128],
                  sb["BF1"][:, 1028 + ich: 1028 + ich + 1], sb["c_idmf"][:, :],
                  inc=pe)
    v_tr = pe.v
    yield _w("dve", pe, v_tr)
    yield _op("dve", "tensor_copy", sb["BF1"][0:1, 1032: 1032 + 256],
              ps["p5"][0:1, 0:256], inc=dve)
    yield _op("dve", "tensor_copy", sb["BF1"][0:1, 1288:1544],
              ps["p5"][0:1, 256:512], inc=dve)
    v_rw = dve.v
    bk.done("p5", dve)
    for w_ in bk.wait("p5"):
        yield w_
    yield _w("pe", dve, v_rw)
    yield _op("pe", "matmul", ps["p5"][:, 0:256], sb["c_onesf"][0:1, 0:128],
              sb["BF1"][0:1, 1032:1288], start=True, stop=True, inc=pe)
    v_bm = pe.v
    yield _w("dve", pe, v_bm)
    yield _op("dve", "tensor_copy", sb["BF1"][:, 512:768], ps["p5"][:, 0:256], inc=dve)
    bk.done("p5", dve)
    for w_ in bk.wait("p5"):
        yield w_
    yield _w("pe", dve, dve.v)
    yield _op("pe", "matmul", ps["p5"][:, 0:256], sb["c_onesf"][0:1, 0:128],
              sb["BF1"][0:1, 1288:1544], start=True, stop=True, inc=pe)
    v_bz = pe.v
    yield _w("dve", pe, v_bz)
    yield _op("dve", "tensor_copy", sb["BF1"][:, 768:1024],
              ps["p5"][:, 0:256], inc=dve)
    bk.done("p5", dve)


def _b_mini_attn(nc, ap, sb, ps, ctr, ctx, l, zsh, v_wqkv, v_wo):
    pe, dve, act = ctr["pe"], ctr["dve"], ctr["act"]
    qx = ctr["qx"]
    bk = ctx["banks"]

    yield _w("pool", dve, dve.v)
    yield _w("pool", ctr["qt"], ctr["qt"].v)
    for i_, nm in enumerate(("cs", "ss")):
        yield _dma("pool", sb["BT"][:, 2040 + i_: 2041 + i_], ap[nm][:, 271:272],
                   ctr["qt"], allow_slow_non_contiguous=True)
    for i_, nm in enumerate(("ci", "si")):
        yield _dma("pool", sb["BT"][:, 2042 + i_: 2043 + i_], ap[nm][:, 15:16],
                   ctr["qt"], allow_slow_non_contiguous=True)
    v_t1 = ctr["qt"].v

    for src_i, (woff, t_, dst0) in enumerate(((0, -1, 0), (1024, -257, 8), (2048, -257, 16))):
        for w_ in bk.wait("p4"):
            yield w_
        yield _w("pe", dve, dve.v)
        yield _w("pe", ctr["qw"], v_wqkv)
        for mc in range(CBN):
            for kc in range(CBN):
                yield _op("pe", "matmul", ps["p4"][:, mc: mc + 1],
                          sb["wpB"][:, kc * 3072 + woff + mc * 128: kc * 3072 + woff + mc * 128 + 128],
                          zsh(kc, t_, 1), start=(kc == 0), stop=(kc == CBN - 1), inc=pe)
        v_m = pe.v
        yield _w("dve", pe, v_m)
        yield _op("dve", "tensor_copy", sb["b_at"][:, dst0: dst0 + 8],
                  ps["p4"][:, 0:8], inc=dve)
        bk.done("p4", dve)
    v_cols = dve.v

    for which, (c0, tb, scol) in enumerate(((0, 2040, 0), (8, 2042, 48))):
        for w_ in bk.wait("p2"):
            yield w_
        yield _w("act", pe, pe.v)
        yield _w("act", dve, dve.v)
        yield _op("act", "activation", sb["sq"][:, 0:8],
                  sb["b_at"][:, c0: c0 + 8], AF.Square, inc=act)
        v_s = act.v
        yield _w("pe", act, v_s)
        for mc in range(CBN):
            yield _op("pe", "matmul", ps["p2"][0:16, 0:1],
                      sb["c_esel"][:, mc * 16: mc * 16 + 16], sb["sq"][:, mc: mc + 1],
                      start=(mc == 0), stop=(mc == CBN - 1), inc=pe)
        v_n = pe.v
        yield _w("act", pe, v_n)
        yield _op("act", "activation", sb["BF1"][0:16, 0:1], ps["p2"][0:16, 0:1],
                  AF.Sqrt, inc=act)
        v_q = act.v
        bk.done("p2", act)
        yield _w("dve", act, v_q)
        yield _op("dve", "tensor_scalar", sb["BF1"][0:16, 0:1], sb["BF1"][0:16, 0:1],
                  1e-12, None, ALU.max, inc=dve)
        yield _w("dve", dve, dve.v)
        yield _op("dve", "reciprocal", sb["BF1"][0:16, 0:1], sb["BF1"][0:16, 0:1], inc=dve)
        yield _w("dve", dve, dve.v)
        yield _op("dve", "tensor_copy", sb["sq"][0:16, 0:1], sb["BF1"][0:16, 0:1], inc=dve)
        v_r = dve.v
        for mc in range(CBN):
            pn = "p3" if mc % 2 == 0 else "p2"
            for w_ in bk.wait(pn):
                yield w_
            yield _w("pe", dve, v_r)
            yield _op("pe", "matmul", ps[pn][:, 0:1],
                      sb["c_rsel"][:, mc * 128: mc * 128 + 128], sb["sq"][0:16, 0:1],
                      start=True, stop=True, inc=pe)
            v_b = pe.v
            yield _w("dve", pe, v_b)
            yield _op("dve", "scalar_tensor_tensor", sb["b_at"][:, c0 + mc: c0 + mc + 1],
                      sb["b_at"][:, c0 + mc: c0 + mc + 1],
                      sb["prm"][:, scol + l * 8 + mc: scol + l * 8 + mc + 1],
                      ps[pn][:, 0:1], ALU.mult, ALU.mult, inc=dve)
            bk.done(pn, dve)
        for mc in range(CBN):
            pn = "p3" if mc % 2 == 0 else "p2"
            for w_ in bk.wait(pn):
                yield w_
            yield _w("pe", dve, dve.v)
            yield _op("pe", "matmul", ps[pn][:, 0:1], sb["c_prot"][:, :],
                      sb["b_at"][:, c0 + mc: c0 + mc + 1], start=True, stop=True, inc=pe)
            v_p = pe.v
            yield _w("dve", pe, v_p)
            yield _w("dve", ctr["qt"], v_t1)
            yield _op("dve", "tensor_mul", sb["BF1"][:, 0:1],
                      sb["b_at"][:, c0 + mc: c0 + mc + 1], sb["BT"][:, tb: tb + 1], inc=dve)
            yield _op("dve", "tensor_mul", sb["BF1"][:, 1:2],
                      ps[pn][:, 0:1], sb["BT"][:, tb + 1: tb + 2], inc=dve)
            yield _w("dve", dve, dve.v)
            yield _op("dve", "tensor_add", sb["b_at"][:, c0 + mc: c0 + mc + 1],
                      sb["BF1"][:, 0:1], sb["BF1"][:, 1:2], inc=dve)
            bk.done(pn, dve)
    v_qk1 = dve.v

    for h in range(HF):
        mc = h // 2
        hrow = (h % 2) * 64
        for w_ in bk.wait("p0"):
            yield w_
        yield _w("pe", dve, v_qk1)
        yield _op("pe", "matmul", ps["p0"][0:1, 0:256],
                  sb["b_at"][hrow:hrow + 64, mc: mc + 1],
                  sb["b_kr"][hrow:hrow + 64, mc * 768: mc * 768 + 256],
                  start=True, stop=True, inc=pe)
        yield _op("pe", "matmul", ps["p0"][0:1, 256:257],
                  sb["b_at"][hrow:hrow + 64, mc: mc + 1],
                  sb["b_at"][hrow:hrow + 64, 8 + mc: 8 + mc + 1],
                  start=True, stop=True, inc=pe)
        v_sim = pe.v
        yield _w("dve", pe, v_sim)
        yield _op("dve", "tensor_reduce", sb["BF1"][0:1, 0:1], ps["p0"][0:1, 0:257],
                  mybir.AxisListType.X, ALU.max, negate=True, inc=dve)
        v_mx = dve.v
        yield _w("act", dve, v_mx)
        yield _op("act", "activation", sb["sq"][0:1, 0:257], ps["p0"][0:1, 0:257],
                  AF.Exp, bias=sb["BF1"][0:1, 0:1],
                  accum_out=sb["BF1"][0:1, 1:2], inc=act)
        v_e = act.v
        bk.done("p0", act)
        yield _w("dve", act, v_e)
        yield _op("dve", "reciprocal", sb["BF1"][0:1, 1:2], sb["BF1"][0:1, 1:2], inc=dve)
        yield _w("dve", dve, dve.v)
        for w_ in bk.wait("pb0"):
            yield w_
        yield _w("pe", dve, dve.v)
        yield _op("pe", "transpose", ps["pb0"][0:128, 0:1], sb["sq"][0:1, 0:128],
                  sb["c_idm"][0:1, 0:1], inc=pe)
        yield _op("pe", "transpose", ps["pb0"][0:128, 2:3], sb["sq"][0:1, 128:256],
                  sb["c_idm"][0:1, 0:1], inc=pe)
        v_tr = pe.v
        yield _w("dve", pe, v_tr)
        yield _op("dve", "tensor_copy", sb["sq2"][0:128, 0:1], ps["pb0"][0:128, 0:1], inc=dve)
        yield _op("dve", "tensor_copy", sb["sq2"][0:128, 2:3], ps["pb0"][0:128, 2:3], inc=dve)
        v_atc = dve.v
        bk.done("pb0", dve)
        for w_ in bk.wait("p1"):
            yield w_
        yield _w("pe", dve, v_atc)
        for sl in range(2):
            yield _op("pe", "matmul", ps["p1"][hrow:hrow + 64, 0:1],
                      sb["b_vr"][:, sl * 1024 + h * 64: sl * 1024 + h * 64 + 64],
                      sb["sq2"][0:128, 2 * sl: 2 * sl + 1],
                      start=(sl == 0), stop=(sl == 1), inc=pe,
                      tile_position=(0, hrow))
        v_av = pe.v
        for w_ in bk.wait("p3"):
            yield w_
        yield _w("pe", act, v_e)
        yield _op("pe", "matmul", ps["p3"][hrow:hrow + 64, 0:1], sb["c_ones"][0:1, 0:64],
                  sb["sq"][0:1, 256:257], start=True, stop=True, inc=pe,
                  tile_position=(0, hrow))
        v_bx = pe.v
        yield _w("dve", pe, v_bx)
        yield _op("dve", "tensor_copy", sb["BF1"][hrow:hrow + 64, 4:5],
                  ps["p3"][hrow:hrow + 64, 0:1], inc=dve)
        v_bxs = dve.v
        bk.done("p3", dve)
        yield _w("dve", pe, v_av)
        yield _w("dve", dve, v_bxs)
        yield _op("dve", "tensor_mul", sb["BF1"][hrow:hrow + 64, 5:6],
                  sb["b_at"][hrow:hrow + 64, 16 + mc: 16 + mc + 1],
                  sb["BF1"][hrow:hrow + 64, 4:5], inc=dve)
        yield _w("dve", dve, dve.v)
        yield _op("dve", "tensor_add", sb["BF1"][hrow:hrow + 64, 5:6],
                  sb["BF1"][hrow:hrow + 64, 5:6], ps["p1"][hrow:hrow + 64, 0:1], inc=dve)
        v_o = dve.v
        bk.done("p1", dve)
        for w_ in bk.wait("p3"):
            yield w_
        yield _w("dve", dve, dve.v)
        yield _op("dve", "tensor_copy", sb["sq"][0:1, 300:301], sb["BF1"][0:1, 1:2], inc=dve)
        v_zb = dve.v
        yield _w("pe", dve, v_zb)
        yield _op("pe", "matmul", ps["p3"][hrow:hrow + 64, 0:1], sb["c_ones"][0:1, 0:64],
                  sb["sq"][0:1, 300:301], start=True, stop=True, inc=pe,
                  tile_position=(0, hrow))
        v_zbc = pe.v
        yield _w("dve", pe, v_zbc)
        yield _op("dve", "tensor_mul", sb["b_at"][hrow:hrow + 64, 24 + mc: 24 + mc + 1],
                  sb["BF1"][hrow:hrow + 64, 5:6], ps["p3"][hrow:hrow + 64, 0:1], inc=dve)
        bk.done("p3", dve)
    v_o1 = dve.v

    for cb in range(CBN):
        half, j = cb // 4, cb % 4
        pn = "p4" if cb % 2 == 0 else "p5"
        for w_ in bk.wait(pn):
            yield w_
        yield _w("pe", ctr["qw"], v_wo)
        yield _w("pe", dve, v_o1)
        for kc in range(CBN):
            yield _op("pe", "matmul", ps[pn][:, 0:1],
                      sb["wo_p"][:, half * 4096 + kc * 512 + j * 128: half * 4096 + kc * 512 + j * 128 + 128],
                      sb["b_at"][:, 24 + kc: 24 + kc + 1],
                      start=(kc == 0), stop=(kc == CBN - 1), inc=pe)
        v_wm = pe.v
        yield _w("dve", pe, v_wm)
        yield _op("dve", "tensor_add", sb["hstash"][:, cb: cb + 1],
                  sb["z_bf"][:, cb * ZW + 271: cb * ZW + 272], ps[pn][:, 0:1], inc=dve)
        bk.done(pn, dve)


def _b_ffn(nc, ap, sb, ps, ctr, ctx, l):
    pe, dve, act, pool = ctr["pe"], ctr["dve"], ctr["act"], ctr["pool"]
    qw, qx, qs_, ql, qz, qy, cc = (ctr[k] for k in ("qw", "qx", "qs", "ql", "qz", "qy", "cc"))
    bk = ctx["banks"]
    final = (l == NL - 1)

    def zsh(kc, t, w):
        off = -1 if kc >= 4 else 0
        return sb["z_bf"][:, kc * ZW + HALO_F + t + off: kc * ZW + HALO_F + t + off + w]

    for third in range(3):
        js = THIRDS[third]
        J = len(js)
        j0 = js[0]
        W = sum(JW[j] for j in js)
        w1len = CBN * 2 * W
        yield _w("pool", pe, pe.v)
        for kc in range(CBN):
            yield _dma("pool", sb["wpB"][:, kc * 2 * W: kc * 2 * W + W],
                       ap["w1"][l, kc * 128:(kc + 1) * 128, JOFF[j0]: JOFF[j0] + W], qw)
            yield _dma("pool", sb["wpB"][:, kc * 2 * W + W: kc * 2 * W + 2 * W],
                       ap["w1"][l, kc * 128:(kc + 1) * 128, DI + JOFF[j0]: DI + JOFF[j0] + W], qw)
        for jj in range(J):
            j = js[jj]
            yield _dma("pool", sb["wpB"][0:JW[j], w1len + jj * 1024: w1len + jj * 1024 + 1024],
                       ap["w2"][l, JOFF[j]: JOFF[j] + JW[j], :], qw)
        v_w = qw.v

        yield _w("pe", qw, v_w)
        yield _w("pe", ctr["qt"], ctr["qt"].v)
        yield _w("pe", dve, dve.v)
        yield _w("pe", act, act.v)
        for w_ in bk.wait("p0"):
            yield w_
        jcol = {}
        for jj in range(J):
            j = js[jj]
            for ag in range(2):
                coff = sum(JW[jx] for jx in js[:jj])
                for kc in range(CBN):
                    yield _op("pe", "matmul",
                              ps["p0"][0:JW[j], (jj * 2 + ag): (jj * 2 + ag) + 1],
                              sb["wpB"][:, kc * 2 * W + ag * W + coff: kc * 2 * W + ag * W + coff + JW[j]],
                              sb["lnb_b"][:, kc: kc + 1],
                              start=(kc == 0), stop=(kc == CBN - 1), inc=pe)
        v_bv = pe.v
        yield _w("dve", pe, v_bv)
        for jj in range(J):
            j = js[jj]
            for ag in range(2):
                yield _op("dve", "scalar_tensor_tensor",
                          sb["prm"][0:JW[j], 300 + jj * 2 + ag: 300 + jj * 2 + ag + 1],
                          ps["p0"][0:JW[j], (jj * 2 + ag): (jj * 2 + ag) + 1], 1.0,
                          sb["prm"][0:JW[j], 240 + ag * 22 + j: 240 + ag * 22 + j + 1],
                          ALU.mult, ALU.add, inc=dve)
        v_bvs = dve.v
        bk.done("p0", dve)

        for T in range(8):
            t0 = T * 512
            ops, v_bc = _ln_stats(sb, ps, ctr, bk,
                                  lambda kc, t0=t0: zsh(kc, t0, 512),
                                  512, "p0", "p1")
            for o_ in ops:
                yield o_
            yield _w("dve", pe, v_bc)
            for kc in range(CBN):
                yield _op("dve", "tensor_sub", sb["BF1"][:, 0:512],
                          zsh(kc, t0, 512), ps["p0"][:, 0:512], inc=dve)
                yield _w("dve", dve, dve.v)
                yield _op("dve", "scalar_tensor_tensor",
                          sb["b_xn"][:, kc * 512: kc * 512 + 512],
                          sb["BF1"][:, 0:512], sb["prm"][:, 96 + l * 8 + kc: 96 + l * 8 + kc + 1],
                          ps["p1"][:, 0:512], ALU.mult, ALU.mult, inc=dve)
            v_xn = dve.v
            bk.done("p0", dve)
            bk.done("p1", dve)

            for sweep in range(2):
                ocs = range(4) if sweep == 0 else range(4, 8)
                for jj in range(J):
                    j = js[jj]
                    coff = sum(JW[jx] for jx in js[:jj])
                    if sweep == 0:
                        for w_ in bk.wait("p0") + bk.wait("p1"):
                            yield w_
                        yield _w("pe", dve, v_xn)
                        yield _w("pe", dve, v_bvs)
                        for kc in range(CBN):
                            yield _op("pe", "matmul", ps["p0"][0:JW[j], 0:512],
                                      sb["wpB"][:, kc * 2 * W + coff: kc * 2 * W + coff + JW[j]],
                                      sb["b_xn"][:, kc * 512: kc * 512 + 512],
                                      start=(kc == 0), stop=(kc == CBN - 1), inc=pe)
                        for kc in range(CBN):
                            yield _op("pe", "matmul", ps["p1"][0:JW[j], 0:512],
                                      sb["wpB"][:, kc * 2 * W + W + coff: kc * 2 * W + W + coff + JW[j]],
                                      sb["b_xn"][:, kc * 512: kc * 512 + 512],
                                      start=(kc == 0), stop=(kc == CBN - 1), inc=pe)
                        v_h = pe.v
                        yield _w("act", pe, v_h)
                        yield _op("act", "activation", sb["sq"][0:JW[j], 0:512],
                                  ps["p1"][0:JW[j], 0:512], AF.Gelu,
                                  bias=sb["prm"][0:JW[j], 300 + jj * 2 + 1: 300 + jj * 2 + 2],
                                  inc=act)
                        v_g = act.v
                        bk.done("p1", act)
                        yield _w("dve", act, v_g)
                        yield _op("dve", "scalar_tensor_tensor",
                                  sb["pcc"][0:JW[j], jj * 512: jj * 512 + 512],
                                  ps["p0"][0:JW[j], 0:512],
                                  sb["prm"][0:JW[j], 300 + jj * 2: 300 + jj * 2 + 1],
                                  sb["sq"][0:JW[j], 0:512], ALU.add, ALU.mult, inc=dve)
                        v_p = dve.v
                        bk.done("p0", dve)
                    else:
                        v_p = dve.v
                    yield _w("pe", dve, v_p)
                    if jj == 0:
                        for oc in ocs:
                            pn = f"p{2 + oc % 4}"
                            for w_ in bk.wait(pn):
                                yield w_
                    for oc in ocs:
                        pn = f"p{2 + oc % 4}"
                        yield _op("pe", "matmul", ps[pn][:, 0:512],
                                  sb["wpB"][0:JW[j], w1len + jj * 1024 + oc * 128: w1len + jj * 1024 + oc * 128 + 128],
                                  sb["pcc"][0:JW[j], jj * 512: jj * 512 + 512],
                                  start=(jj == 0), stop=(jj == J - 1), inc=pe)
                v_w2 = pe.v
                for oc in ocs:
                    pn = f"p{2 + oc % 4}"
                    yield _w("sync", qz, qz.v)
                    yield _w("sync", dve, dve.v)
                    if final and third == 2:
                        yield _w("sync", qy, qy.v)
                    yield _dma("sync", sb["zt2"][:, (oc % 2) * 512: (oc % 2) * 512 + 512],
                               ap["zf32"][oc, :, t0:t0 + 512], qx)
                    v_zl = qx.v
                    yield _w("dve", pe, v_w2)
                    yield _w("dve", qx, v_zl)
                    if third == 2:
                        yield _op("dve", "scalar_tensor_tensor",
                                  sb["zt2"][:, (oc % 2) * 512: (oc % 2) * 512 + 512],
                                  ps[pn][:, 0:512],
                                  sb["prm"][:, 192 + l * 8 + oc: 192 + l * 8 + oc + 1],
                                  sb["zt2"][:, (oc % 2) * 512: (oc % 2) * 512 + 512],
                                  ALU.add, ALU.add, inc=dve)
                    else:
                        yield _op("dve", "tensor_add",
                                  sb["zt2"][:, (oc % 2) * 512: (oc % 2) * 512 + 512],
                                  sb["zt2"][:, (oc % 2) * 512: (oc % 2) * 512 + 512],
                                  ps[pn][:, 0:512], inc=dve)
                    v_zu = dve.v
                    bk.done(pn, dve)
                    if final and third == 2:
                        yield _w("pool", dve, v_zu)
                        yield _w("pool", qy, qy.v)
                        yield _dma("pool", ap["y_cm"][oc, :, t0:t0 + 512],
                                   sb["zt2"][:, (oc % 2) * 512: (oc % 2) * 512 + 512],
                                   qy, accum_op=ALU.add)
                    else:
                        yield _w("sync", dve, v_zu)
                        yield _dma("sync", ap["zf32"][oc, :, t0:t0 + 512],
                                   sb["zt2"][:, (oc % 2) * 512: (oc % 2) * 512 + 512], qz)

    if not final:
        yield _w("pool", qz, qz.v)
        yield _w("pool", pe, pe.v)
        yield _w("pool", ctr["qt"], ctr["qt"].v)
        for cb in range(CBN):
            yield _dma("pool", sb["z_bf"][:, cb * ZW + HALO_F: cb * ZW + HALO_F + TOK],
                       ap["zf32"][cb], ctr["qt"])
        v_rf = ctr["qt"].v
        yield _w("sync", ctr["qt"], v_rf)
        for cb in range(CBN):
            yield _dma("sync", ap["ag_z_i"][:, cb * 258:(cb + 1) * 258],
                       sb["z_bf"][:, cb * ZW + HALO_F + TOK - 258: cb * ZW + HALO_F + TOK], qs_)
        yield _w("pool", qs_, qs_.v)
        cc.inc(1)
        def cc_fn(e, a=ap, h=cc.h):
            e.collective_compute(
                "AllGather", ALU.bypass,
                replica_groups=[[0, 1], [2, 3], [4, 5], [6, 7]],
                ins=[a["ag_z_i"].tensor.ap().opt()],
                outs=[a["ag_z_o"].tensor.ap().opt()],
            ).then_inc(h, 1)
        yield ("pool", cc_fn)


def kernel(**inputs):
    in_maps = _stage(inputs)
    nc = _build_nc()
    res = run_bass_kernel_spmd(nc, in_maps, core_ids=list(range(8)))
    out = np.zeros((B, L, D), np.float32)
    for c in range(8):
        b, s = c // 2, c % 2
        y = res.results[c]["y_cm"]
        out[b, s * TOK:(s + 1) * TOK, :] = y.reshape(D, TOK).T
    return out
